# version 07
# baseline (speedup 1.0000x reference)
import numpy as np
import ml_dtypes
from contextlib import ExitStack

import concourse.bass as bass
import concourse.mybir as mybir
from concourse.bass_utils import run_bass_kernel_spmd

bf16 = ml_dtypes.bfloat16
F32 = mybir.dt.float32
BF16 = mybir.dt.bfloat16
U8 = mybir.dt.uint8
AF = mybir.ActivationFunctionType
ALU = mybir.AluOpType

ENGINES = ("tensor", "vector", "scalar", "gpsimd", "sync")

C_LN_EPS = float(np.float32(1e-5))
C_EXP_BIAS = float(np.float32(-24.0))
C_Q_SC = float(np.float32(128.0 / 2048.0))
C_Q_BI = float(np.float32(128.0 * 1e-6))
C_K_SC = float(np.float32(1.0 / 512.0))
C_K_BI = float(np.float32(1e-6))
C_H_SC = float(np.float32(1.0 / 8192.0))
C_H_BI = float(np.float32(np.finfo(np.float32).eps))
C_Q64 = float(np.float32(1.0 / 64.0))
C_CLIP8 = float(np.asarray(1e-8, dtype=np.float32).astype(bf16).astype(np.float32))
CONSTS = [C_LN_EPS, C_EXP_BIAS, C_Q_SC, C_Q_BI, C_K_SC, C_K_BI, C_H_SC, C_H_BI, C_Q64]

_DBG = ()


class KB:

    def __init__(self):
        self.nc = bass.Bass("TRN2", target_bir_lowering=False)
        self.ops = {e: [] for e in ENGINES}
        self.n = {s: 0 for s in ("sG", "sYL", "sYS", "sT", "sV", "sS", "sC")}
        self.trk = {}
        self.pend = {e: [] for e in ENGINES}

    def wait(self, eng, sem, val):
        if val and val > 0:
            self.pend[eng].append((sem, val))

    def emit(self, eng, fn, inc=None, amt=1):
        waits, self.pend[eng] = self.pend[eng], []
        post = None
        if inc is not None:
            self.n[inc] += amt
            post = self.n[inc]
        self.ops[eng].append((waits, fn, inc, amt))
        return post

    def dma(self, queue, dst, src, sem):
        return self.emit(queue, lambda e, d=dst, s=src: e.dma_start(d, s), inc=sem, amt=16)

    def mm(self, out, lhsT, rhs, start, stop):
        import traceback
        site = traceback.extract_stack()[-2]
        def f(e, o=out, l=lhsT, r=rhs, a=start, b=stop, si=site):
            try:
                return e.matmul(o, l, r, start=a, stop=b)
            except Exception:
                print("MM FAIL at", si.filename, si.lineno, "lhsT", l.shape, "rhs", r.shape,
                      "out", o.shape)
                raise
        return self.emit("tensor", f, inc="sT")

    def barrier(self):
        snap = [(s, v) for s, v in self.n.items() if v > 0]
        for e in ENGINES:
            self.pend[e].extend(snap)

    def replay(self, block, sems):
        def mk(stream):
            def body(e):
                for waits, fn, inc, amt in stream:
                    for s, v in waits:
                        e.wait_ge(sems[s], v)
                    inst = fn(e)
                    if inc is not None:
                        if inst is None:
                            inst = e.nop()
                        inst.then_inc(sems[inc], amt)
            return body
        for name in ENGINES:
            if self.ops[name]:
                getattr(block, name)(mk(self.ops[name]))


def build(dbg=()):
    kb = KB()
    nc = kb.nc
    T = kb.trk

    xt = nc.dram_tensor("xt", [2048, 4096], F32, kind="ExternalInput")
    wq = nc.dram_tensor("wq", [16, 128, 16, 128], F32, kind="ExternalInput")
    wk = nc.dram_tensor("wk", [4, 128, 16, 128], F32, kind="ExternalInput")
    wv = nc.dram_tensor("wv", [4, 128, 16, 128], F32, kind="ExternalInput")
    wo = nc.dram_tensor("wo", [16, 128, 16, 128], F32, kind="ExternalInput")
    wshard = nc.dram_tensor("wshard", [24, 128, 2048], F32, kind="ExternalInput")
    cost = nc.dram_tensor("cost", [128, 4096], F32, kind="ExternalInput")
    sint = nc.dram_tensor("sint", [128, 4096], F32, kind="ExternalInput")
    ones1 = nc.dram_tensor("ones1", [128, 1], F32, kind="ExternalInput")
    onesk1 = nc.dram_tensor("onesk1", [1, 128], F32, kind="ExternalInput")
    sel2 = nc.dram_tensor("sel2", [2, 128], F32, kind="ExternalInput")
    segc = nc.dram_tensor("segc", [128, 2], F32, kind="ExternalInput")
    eye = nc.dram_tensor("eye", [128, 128], F32, kind="ExternalInput")
    rmat = nc.dram_tensor("rmat", [128, 128], F32, kind="ExternalInput")
    cvec = nc.dram_tensor("cvec", [128, 16], F32, kind="ExternalInput")
    o = nc.dram_tensor("o", [2048, 2048], F32, kind="ExternalOutput")

    def scratch(name, shape, dt_, flag):
        return nc.dram_tensor(name, shape, dt_,
                              kind="ExternalOutput" if flag in dbg else "Internal")
    s_xn = scratch("s_xn", [16, 128, 2048], BF16, "xn")
    s_enc = scratch("s_enc", [16, 128, 2048], BF16, "enc")
    s_attn = scratch("s_attn", [16, 128, 2048], BF16, "attn")
    q_in = nc.dram_tensor("q_in", [24, 128, 2048], BF16)
    q_out = nc.dram_tensor("q_out", [192, 128, 2048], BF16, addr_space="Shared")
    bar_in = nc.dram_tensor("bar_in", [1, 128], F32)
    bar_out = nc.dram_tensor("bar_out", [1, 128], F32, addr_space="Shared")
    d_qo = (nc.dram_tensor("d_qo", [192, 128, 2048], BF16, kind="ExternalOutput")
            if "wgu" in dbg else None)
    d_kqv = None
    if "kqv" in dbg:
        d_kqv = (nc.dram_tensor("d_kp", [4, 128, 4096], BF16, kind="ExternalOutput"),
                 nc.dram_tensor("d_qp", [16, 128, 2048], BF16, kind="ExternalOutput"),
                 nc.dram_tensor("d_vt", [32, 128, 512], BF16, kind="ExternalOutput"))
    d_h = (nc.dram_tensor("d_h", [64, 128, 1024], BF16, kind="ExternalOutput")
           if "h" in dbg else None)

    es = ExitStack()
    sbt = lambda n_, s_, d_: es.enter_context(nc.sbuf_tensor(n_, s_, d_))

    AR1 = sbt("AR1", [128, 65536], U8)
    AR2 = sbt("AR2", [128, 65536], U8)
    AR3 = sbt("AR3", [128, 32768], U8)
    AR4 = sbt("AR4", [128, 32768], U8)
    AR5 = sbt("AR5", [128, 14336], U8)

    def view(ar, off, shape, dt_):
        nbytes = int(np.prod(shape[1:])) * mybir.dt.size(dt_)
        assert off % 32 == 0 and off + nbytes <= int(ar.shape[1]), (off, nbytes, ar.name)
        flat = ar.reshape([128, int(ar.shape[1])]).bitcast(dt_)
        elem = mybir.dt.size(dt_)
        ap = flat[0:shape[0], off // elem: off // elem + nbytes // elem]
        if len(shape) == 3:
            ap = ap.rearrange("p (a b) -> p a b", a=shape[1], b=shape[2])
        return ap

    ones1_b = sbt("ones1_b", [128, 1], BF16)
    onesk1_f = sbt("onesk1_f", [1, 128], F32)
    sel2_b = sbt("sel2_b", [2, 128], BF16)
    segc_b = sbt("segc_b", [128, 2], BF16)
    eye_b = sbt("eye_b", [128, 128], BF16)
    rmat_b = sbt("rmat_b", [128, 128], BF16)
    cv = sbt("cv", [128, 16], F32)
    for i, c in enumerate(CONSTS):
        nc.const_aps.register(F32, c, cv[:, i:i + 1])

    psA = es.enter_context(nc.psum_tensor("psA", [128, 2048], F32))
    psB = es.enter_context(nc.psum_tensor("psB", [128, 1536], F32))
    psT = es.enter_context(nc.psum_tensor("psT", [128, 1024], BF16))

    for dst, src in ((ones1_b, ones1), (onesk1_f, onesk1), (sel2_b, sel2),
                     (segc_b, segc), (eye_b, eye), (rmat_b, rmat),
                     (cv, cvec)):
        kb.dma("gpsimd", dst[:, :], src.ap(), "sG")
    kb.barrier()

    xt_r = xt.ap().rearrange("(kc p) t -> p kc t", p=128)

    kp = view(AR3, 0, [128, 4, 4096], BF16)
    vt = view(AR4, 0, [128, 32, 512], BF16)
    qp = view(AR2, 0, [128, 16, 2048], BF16)
    xnh = view(AR1, 0, [128, 16, 2048], BF16)
    wbuf = [view(AR5, 0, [128, 2048], BF16), view(AR5, 4096, [128, 2048], BF16)]
    sqt = [view(AR5, 8192, [128, 512], BF16), view(AR5, 9216, [128, 512], BF16)]
    m_row = view(AR5, 10240, [1, 512], F32)
    a_row = view(AR5, 12288, [1, 512], F32)
    cos_sl = view(AR5, 0, [128, 512], F32)
    sin_sl = view(AR5, 2048, [128, 512], F32)
    se_t = view(AR5, 4096, [128, 512], BF16)
    t1_t = view(AR5, 5120, [128, 512], BF16)
    t2_t = view(AR5, 6144, [128, 512], BF16)
    rope_sq = view(AR5, 7168, [128, 512], BF16)

    qbuf = [view(AR2, 51200, [128, 2048], BF16), view(AR2, 55296, [128, 2048], BF16)]
    r_tQ = view(AR2, 59392, [128, 512], BF16)
    sc_bfQ = [view(AR2, 60416, [2, 512], BF16), view(AR2, 61440, [2, 512], BF16)]
    scB = view(AR2, 62464, [128, 512], BF16)
    rcp128 = view(AR2, 63488, [128, 512], F32)

    QPAR = [0]

    def quant_tile(w_ap):
        qq = QPAR[0]
        QPAR[0] ^= 1
        sc_t = sc_bfQ[qq]
        pst = psA[:, 1024 + qq * 512: 1536 + qq * 512]
        kb.wait("scalar", "sV", T.get("q_done", 0))
        av = kb.emit("scalar", lambda e, w=w_ap: e.activation(
            r_tQ[:, :], w, AF.Abs), inc="sS")
        kb.wait("tensor", "sS", av)
        kb.wait("tensor", "sS", T.get("q_segfree", 0))
        sv = kb.mm(psB[0:2, 1024:1536], segc_b[:, :], r_tQ[:, :], True, True)
        kb.wait("scalar", "sT", sv)
        kb.wait("scalar", "sT", T.get("q_bc%d" % qq, 0))
        scv = kb.emit("scalar", lambda e, d=sc_t: e.activation(
            d[:, :], psB[0:2, 1024:1536], AF.Copy, scale=C_Q64), inc="sS")
        T["q_segfree"] = scv
        kb.wait("tensor", "sS", scv)
        kb.wait("tensor", "sV", T.get("q_pst%d" % qq, 0))
        bsv = kb.mm(pst, sel2_b[:, :], sc_t[:, :], True, True)
        T["q_bc%d" % qq] = bsv
        kb.wait("vector", "sT", bsv)
        cmv = kb.emit("vector", lambda e, p=pst: e.tensor_scalar(
            scB[:, :], p, C_CLIP8, None, ALU.max), inc="sV")
        T["q_pst%d" % qq] = cmv
        kb.emit("vector", lambda e: e.reciprocal(rcp128[:, :], scB[:, :]), inc="sV")
        kb.emit("vector", lambda e, w=w_ap: e.tensor_tensor(
            r_tQ[:, :], w, rcp128[:, :], ALU.mult), inc="sV")
        kb.emit("vector", lambda e: e.tensor_scalar(
            r_tQ[:, :], r_tQ[:, :], 192.0, None, ALU.add), inc="sV")
        kb.emit("vector", lambda e: e.tensor_scalar(
            r_tQ[:, :], r_tQ[:, :], -192.0, None, ALU.add), inc="sV")
        kb.emit("vector", lambda e: e.tensor_scalar(
            r_tQ[:, :], r_tQ[:, :], -1.0, 1.0, ALU.max, ALU.min), inc="sV")
        wv_ = kb.emit("vector", lambda e, w=w_ap: e.tensor_tensor(
            w, r_tQ[:, :], scB[:, :], ALU.mult), inc="sV")
        T["q_done"] = wv_
        return wv_

    def p1q_block(bi):
        buf = qbuf[bi % 2]
        kb.wait("gpsimd", "sYS", T.get("p1q_st%d" % (bi % 2), 0))
        ld = kb.dma("gpsimd", buf[:, :], wshard.ap()[bi], "sG")
        kb.wait("scalar", "sG", ld)
        qv = None
        for qq in range(4):
            qv = quant_tile(buf[:, qq * 512:(qq + 1) * 512])
        kb.wait("sync", "sV", qv)
        st = kb.dma("sync", q_in.ap()[bi], buf[:, :], "sYS")
        T["p1q_st%d" % (bi % 2)] = st
        T["p1q_store_last"] = st
        T["p1q_V"] = kb.n["sV"]
        T["p1q_T"] = kb.n["sT"]

    P1Q_SCHED = {ji: [ji] for ji in range(18)}
    P1Q_SCHED[18] = [18, 19]
    P1Q_SCHED[19] = [20, 21]
    P1Q_SCHED[20] = [22, 23]

    def ln_half(sh):
        for tb in range(4):
            t0 = sh * 2048 + tb * 512
            tsl = slice(tb * 512, (tb + 1) * 512)
            ldv = kb.dma("gpsimd", xnh[:, :, tsl], xt_r[:, :, t0:t0 + 512], "sG")
            kb.wait("tensor", "sG", ldv)
            kb.wait("tensor", "sV", T.get("ln_psum_free", 0))
            kb.wait("tensor", "sV", T.get("ln_apply", 0))
            sxv = None
            for kc in range(16):
                sxv = kb.mm(psB[0:1, 0:512], ones1_b[:, :], xnh[:, kc, tsl],
                            kc == 0, kc == 15)
            sxxv = None
            for kc in range(16):
                kb.wait("vector", "sG", ldv)
                kb.wait("vector", "sT", T.get("lnsq%d" % (kc % 2), 0))
                sqv = kb.emit("vector",
                              lambda e, a=xnh[:, kc, tsl], q=sqt[kc % 2][:, :]:
                              e.tensor_tensor(q, a, a, ALU.mult), inc="sV")
                kb.wait("tensor", "sV", sqv)
                sxxv = kb.mm(psB[0:1, 512:1024], ones1_b[:, :], sqt[kc % 2][:, :],
                             kc == 0, kc == 15)
                T["lnsq%d" % (kc % 2)] = sxxv
            mtv = kb.emit("vector", lambda e: e.tensor_scalar(
                m_row[:, :], psB[0:1, 0:512], 1.0 / 2048.0, None, ALU.mult), inc="sV")
            kb.wait("vector", "sT", sxxv)
            kb.emit("vector", lambda e: e.tensor_scalar(
                a_row[:, :], psB[0:1, 512:1024], 1.0 / 2048.0, None, ALU.mult), inc="sV")
            kb.emit("vector", lambda e: e.tensor_tensor(
                psB[0:1, 512:1024], m_row[:, :], m_row[:, :], ALU.mult), inc="sV")
            vv = kb.emit("vector", lambda e: e.tensor_tensor(
                a_row[:, :], a_row[:, :], psB[0:1, 512:1024], ALU.subtract), inc="sV")
            T["ln_psum_free"] = vv
            kb.wait("scalar", "sV", vv)
            sv = kb.emit("scalar", lambda e: e.activation(
                a_row[:, :], a_row[:, :], AF.Sqrt, bias=C_LN_EPS), inc="sS")
            kb.wait("tensor", "sV", mtv)
            kb.mm(psA[:, 1024:1536], onesk1_f[:, :], m_row[:, :], True, True)
            kb.wait("tensor", "sS", sv)
            biv = kb.mm(psA[:, 1536:2048], onesk1_f[:, :], a_row[:, :], True, True)
            kb.wait("vector", "sT", biv)
            rv = kb.emit("vector", lambda e: e.reciprocal(
                psA[:, 1536:2048], psA[:, 1536:2048]), inc="sV")
            lastv = None
            for kc in range(16):
                sl = xnh[:, kc, tsl]
                kb.emit("vector", lambda e, a=sl: e.tensor_tensor(
                    a, a, psA[:, 1024:1536], ALU.subtract), inc="sV")
                lastv = kb.emit("vector", lambda e, a=sl: e.tensor_tensor(
                    a, a, psA[:, 1536:2048], ALU.mult), inc="sV")
            T["ln_apply"] = lastv
            if sh == 0:
                kb.wait("sync", "sV", lastv)
                kb.dma("sync", s_xn.ap().transpose([1, 0, 2])[:, :, tsl],
                       xnh[:, :, tsl], "sYS")

    def proj_half(sh):
        jobs = [("k", m) for m in range(4)] + [("v", m) for m in range(4)]
        if sh == 0:
            jobs += [("q", m) for m in range(16)]
        for ji, (kind, m) in enumerate(jobs):
            if sh == 0:
                for bi in P1Q_SCHED.get(ji, []):
                    p1q_block(bi)
            buf = wbuf[ji % 2]
            src = {"k": wk, "v": wv, "q": wq}[kind]
            kb.wait("gpsimd", "sT", T.get("wbuf%d" % (ji % 2), 0))
            ldv = kb.dma("gpsimd", buf[:, :],
                         src.ap()[m].rearrange("p a b -> p (a b)"), "sG")
            banks = [psA[:, 0:512], psA[:, 512:1024], psB[:, 0:512], psB[:, 512:1024]]
            kb.wait("tensor", "sG", ldv)
            for pb_i in range(4):
                pv = T.get("pevac%d" % pb_i)
                if pv is not None:
                    kb.wait("tensor", pv[1], pv[0])
            for kc in range(16):
                for tb in range(4):
                    mmv = kb.mm(banks[tb], buf[:, kc * 128:(kc + 1) * 128],
                                xnh[:, kc, tb * 512:(tb + 1) * 512],
                                kc == 0, kc == 15)
            for tb in range(4):
                tsl = slice(tb * 512, (tb + 1) * 512)
                pb_i = tb
                pbank = banks[tb]
                if kind == "k":
                    dst = kp[:, m, sh * 2048 + tb * 512: sh * 2048 + (tb + 1) * 512]
                    eng = "scalar" if pb_i % 2 == 0 else "vector"
                    sem = "sS" if pb_i % 2 == 0 else "sV"
                    kb.wait(eng, "sT", mmv)
                    if eng == "scalar":
                        ev = (kb.emit(eng, lambda e, d=dst, p=pbank:
                                      e.activation(d, p, AF.Copy), inc=sem), sem)
                    else:
                        ev = (kb.emit(eng, lambda e, d=dst, p=pbank:
                                      e.tensor_copy(d, p), inc=sem), sem)
                elif kind == "q":
                    dst = qp[:, m, tsl]
                    eng = "scalar" if pb_i % 2 == 0 else "vector"
                    sem = "sS" if pb_i % 2 == 0 else "sV"
                    kb.wait(eng, "sT", mmv)
                    if m >= 12:
                        kb.wait(eng, "sYS", T.get("p1q_store_last", 0))
                        kb.wait(eng, "sV", T.get("p1q_V", 0))
                        kb.wait(eng, "sT", T.get("p1q_T", 0))
                    if eng == "scalar":
                        ev = (kb.emit(eng, lambda e, d=dst, p=pbank:
                                      e.activation(d, p, AF.Copy), inc=sem), sem)
                    else:
                        ev = (kb.emit(eng, lambda e, d=dst, p=pbank:
                                      e.tensor_copy(d, p), inc=sem), sem)
                else:
                    vtile = sqt[tb % 2]
                    kb.wait("scalar", "sT", mmv)
                    kb.wait("scalar", "sV", T.get("vtile%d" % (tb % 2), 0))
                    ev0 = kb.emit("scalar", lambda e, d=vtile[:, :], p=pbank:
                                  e.activation(d, p, AF.Copy), inc="sS")
                    ev = (ev0, "sS")
                    kb.wait("tensor", "sS", ev0)
                    kb.wait("tensor", "sV", T.get("psT_free", 0))
                    tv = None
                    for jj in range(4):
                        tv = kb.emit("tensor", lambda e,
                                     oo=psT[:, jj * 128:(jj + 1) * 128],
                                     ii=vtile[:, jj * 128:(jj + 1) * 128]:
                                     e.transpose(oo, ii, eye_b[:, :]), inc="sT")
                    dst = vt[:, sh * 16 + tb * 4: sh * 16 + (tb + 1) * 4,
                             m * 128:(m + 1) * 128]
                    kb.wait("vector", "sT", tv)
                    cpv = kb.emit("vector", lambda e, d=dst,
                                  s=psT[:, 0:512].rearrange("p (a b) -> p a b", a=4, b=128):
                                  e.tensor_copy(d, s), inc="sV")
                    T["psT_free"] = cpv
                    T["vtile%d" % (tb % 2)] = cpv
                T["pevac%d" % pb_i] = ev
            T["wbuf%d" % (ji % 2)] = kb.n["sT"]

    def rope(kind):
        nch = 4 if kind == "k" else 16
        buf3 = kp if kind == "k" else qp
        sc, bi = (C_K_SC, C_K_BI) if kind == "k" else (C_Q_SC, C_Q_BI)
        for tb in range(8 if kind == "k" else 4):
            tsl = slice(tb * 512, (tb + 1) * 512)
            kb.wait("sync", "sV", T.get("rope_cs", 0))
            kb.dma("sync", cos_sl[:, :], cost.ap()[:, tsl], "sYL")
            ls = kb.dma("sync", sin_sl[:, :], sint.ap()[:, tsl], "sYL")
            smv = None
            for ch in range(nch):
                kb.wait("vector", "sT", T.get("rope_sq", 0))
                qv = kb.emit("vector", lambda e, a=buf3[:, ch, tsl], q=rope_sq[:, :]:
                             e.tensor_tensor(q, a, a, ALU.mult), inc="sV")
                kb.wait("tensor", "sV", qv)
                smv = kb.mm(psB[0:1, 0:512], ones1_b[:, :], rope_sq[:, :],
                            ch == 0, ch == nch - 1)
                T["rope_sq"] = smv
            kb.wait("scalar", "sT", smv)
            sv = kb.emit("scalar", lambda e, s=sc, b=bi: e.activation(
                m_row[:, :], psB[0:1, 0:512], AF.Sqrt, bias=b, scale=s), inc="sS")
            kb.wait("vector", "sS", sv)
            rv = kb.emit("vector", lambda e: e.reciprocal(m_row[:, :], m_row[:, :]),
                         inc="sV")
            kb.wait("tensor", "sV", rv)
            kb.wait("tensor", "sS", sv)
            bv = kb.mm(psA[:, 1024:1536], onesk1_f[:, :], m_row[:, :], True, True)
            kb.wait("vector", "sT", bv)
            kb.wait("vector", "sYL", ls)
            cev = kb.emit("vector", lambda e: e.tensor_tensor(
                psB[:, 512:1024], cos_sl[:, :], psA[:, 1024:1536], ALU.mult), inc="sV")
            sev = kb.emit("vector", lambda e: e.tensor_tensor(
                se_t[:, :], sin_sl[:, :], psA[:, 1024:1536], ALU.mult), inc="sV")
            T["rope_cs"] = sev
            for ch in range(nch):
                sl = buf3[:, ch, tsl]
                kb.wait("tensor", "sV", T.get("rope_w", 0))
                rotv = kb.mm(psA[:, (ch % 2) * 512:(ch % 2) * 512 + 512],
                             rmat_b[:, :], sl, True, True)
                kb.wait("vector", "sV", cev)
                kb.emit("vector", lambda e, a=sl: e.tensor_tensor(
                    t1_t[:, :], a, psB[:, 512:1024], ALU.mult), inc="sV")
                kb.wait("vector", "sT", rotv)
                kb.emit("vector", lambda e, c=ch: e.tensor_tensor(
                    t2_t[:, :], psA[:, (c % 2) * 512:(c % 2) * 512 + 512],
                    se_t[:, :], ALU.mult), inc="sV")
                wv_ = kb.emit("vector", lambda e, a=sl: e.tensor_tensor(
                    a, t1_t[:, :], t2_t[:, :], ALU.add), inc="sV")
                T["rope_w"] = wv_

    for sh in (0, 1):
        ln_half(sh)
        kb.barrier()
        proj_half(sh)
        kb.barrier()
    kb.wait("gpsimd", "sYS", T["p1q_store_last"])
    kb.emit("gpsimd", lambda e: e.collective_compute(
        "AllGather", ALU.bypass, replica_groups=[list(range(8))],
        ins=[q_in.ap().opt()], outs=[q_out.ap().opt()]), inc="sC")
    kb.emit("gpsimd", lambda e: e.collective_compute(
        "AllReduce", ALU.add, replica_groups=[list(range(8))],
        ins=[bar_in.ap().opt()], outs=[bar_out.ap().opt()]), inc="sC")
    if d_qo is not None:
        kb.wait("gpsimd", "sC", 2)
        for i_ in range(6):
            kb.dma("gpsimd", d_qo.ap()[i_ * 32:(i_ + 1) * 32],
                   q_out.ap()[i_ * 32:(i_ + 1) * 32], "sG")
    rope("k")
    rope("q")
    kb.barrier()

    if d_kqv is not None:
        d_kp, d_qp, d_vt = d_kqv
        for m in range(4):
            kb.dma("sync", d_kp.ap()[m], kp[:, m, :], "sYS")
        for m in range(16):
            kb.dma("sync", d_qp.ap()[m], qp[:, m, :], "sYS")
        for j in range(32):
            kb.dma("sync", d_vt.ap()[j], vt[:, j, :], "sYS")
        kb.barrier()

    E_s = view(AR1, 0, [128, 32, 512], BF16)
    enc_qb = view(AR1, 32768, [128, 16, 512], BF16)
    rb_s = view(AR5, 0, [128, 512], F32)
    rs_row = view(AR5, 2048, [1, 512], F32)

    def pv_pair(g, n, g_kv):
        kb.wait("tensor", "sS", T["exp%d" % g])
        if g == 0:
            kb.wait("tensor", "sV", T.get("enc_done", 0))
        for jj in range(2):
            j = 2 * g + jj
            kb.mm(psB[:, 0:512], vt[:, j, g_kv * 128:(g_kv + 1) * 128],
                  E_s[:, j, :], j == 0, j == 31)

    pending_norm = []

    def flush_norm():
        while pending_norm:
            fn = pending_norm.pop(0)
            fn()

    for qb in range(4):
        qs = slice(qb * 512, (qb + 1) * 512)
        for n in range(16):
            g_kv = n // 4
            prev_rsA = T.get("rs_endA", 0)
            prev_rsB = T.get("rs_endB", 0)
            for g in range(16):
                grp = psA[:, (g % 2) * 1024:(g % 2) * 1024 + 1024]
                kb.wait("tensor", "sS", T.get("qkgrp%d" % (g % 2), 0))
                for jj in range(2):
                    j = 2 * g + jj
                    kb.mm(grp[:, jj * 512:(jj + 1) * 512],
                          kp[:, g_kv, j * 128:(j + 1) * 128], qp[:, n, qs],
                          True, True)
                if g == 0:
                    flush_norm()
                kb.wait("scalar", "sT", kb.n["sT"])
                kb.wait("scalar", "sT", prev_rsA if g < 8 else prev_rsB)
                ev = kb.emit("scalar", lambda e, g_=g, gr=grp: e.activation(
                    E_s[:, 2 * g_:2 * g_ + 2, :].rearrange("p a b -> p (a b)"),
                    gr, AF.Exp, bias=C_EXP_BIAS), inc="sS")
                T["exp%d" % g] = ev
                T["qkgrp%d" % (g % 2)] = ev
                if g >= 1:
                    pv_pair(g - 1, n, g_kv)
            pv_pair(15, n, g_kv)
            kb.wait("tensor", "sV", T.get("rs_done", 0))
            for j in range(16):
                kb.mm(psB[0:1, 512:1024], ones1_b[:, :], E_s[:, j, :],
                      j == 0, False)
            T["rs_endA"] = kb.n["sT"]
            for j in range(16, 32):
                kb.mm(psB[0:1, 512:1024], ones1_b[:, :], E_s[:, j, :],
                      False, j == 31)
            T["rs_endB"] = kb.n["sT"]
            kb.wait("vector", "sT", kb.n["sT"])
            rv = kb.emit("vector", lambda e: e.tensor_copy(
                rs_row[:, :], psB[0:1, 512:1024]), inc="sV")
            T["rs_done"] = rv

            def mk_norm(n_=n, rv_=rv):
                def do():
                    kb.wait("tensor", "sV", rv_)
                    bv = kb.mm(psB[:, 1024:1536], onesk1_f[:, :], rs_row[:, :],
                               True, True)
                    kb.wait("vector", "sT", bv)
                    kb.emit("vector", lambda e: e.reciprocal(
                        psB[:, 1024:1536], psB[:, 1024:1536]), inc="sV")
                    kb.emit("vector", lambda e: e.tensor_copy(
                        rb_s[:, :], psB[:, 1024:1536]), inc="sV")
                    env = kb.emit("vector", lambda e: e.tensor_tensor(
                        enc_qb[:, n_, :], psB[:, 0:512], rb_s[:, :], ALU.mult),
                        inc="sV")
                    T["enc_done"] = env
                return do
            pending_norm.append(mk_norm())
        flush_norm()
        kb.wait("sync", "sV", kb.n["sV"])
        kb.dma("sync", s_enc.ap().transpose([1, 0, 2])[:, :, qs],
               enc_qb[:, :, :], "sYS")
    kb.barrier()

    enc_full = view(AR1, 0, [128, 16, 2048], BF16)
    wobuf = [view(AR3, 0, [128, 2048], BF16), view(AR3, 4096, [128, 2048], BF16)]
    attn_t = [view(AR5, 8192, [128, 512], BF16), view(AR5, 9216, [128, 512], BF16)]
    lenc = kb.dma("sync", enc_full[:, :, :], s_enc.ap().transpose([1, 0, 2]), "sYL")
    for m in range(16):
        kb.wait("gpsimd", "sT", T.get("wobuf%d" % (m % 2), 0))
        ldv = kb.dma("gpsimd", wobuf[m % 2][:, :],
                     wo.ap()[m].rearrange("p a b -> p (a b)"), "sG")
        banks5 = [psA[:, 0:512], psA[:, 512:1024], psA[:, 1024:1536], psA[:, 1536:2048]]
        kb.wait("tensor", "sG", ldv)
        kb.wait("tensor", "sYL", lenc)
        for qb in range(4):
            kb.wait("tensor", "sS", T.get("aevac%d" % qb, 0))
        mmv = None
        for kc in range(16):
            for qb in range(4):
                mmv = kb.mm(banks5[qb], wobuf[m % 2][:, kc * 128:(kc + 1) * 128],
                            enc_full[:, kc, qb * 512:(qb + 1) * 512], kc == 0, kc == 15)
        for qb in range(4):
            kb.wait("scalar", "sT", mmv)
            kb.wait("scalar", "sYS", T.get("ast%d" % (qb % 2), 0))
            ev = kb.emit("scalar", lambda e, d=attn_t[qb % 2][:, :], p=banks5[qb]:
                         e.activation(d, p, AF.Copy), inc="sS")
            T["aevac%d" % qb] = ev
            kb.wait("sync", "sS", ev)
            stv = kb.dma("sync", s_attn.ap()[m][:, qb * 512:(qb + 1) * 512],
                         attn_t[qb % 2][:, :], "sYS")
            T["ast%d" % (qb % 2)] = stv
        T["wobuf%d" % (m % 2)] = kb.n["sT"]
    kb.barrier()

    h_lo = view(AR1, 0, [128, 32, 1024], BF16)
    h_hi = view(AR2, 0, [128, 32, 1024], BF16)
    xn_th = view(AR4, 0, [128, 16, 1024], BF16)
    gbuf = [view(AR3, i * 4096, [128, 2048], BF16) for i in range(3)]
    ubuf = [view(AR3, 12288 + i * 4096, [128, 2048], BF16) for i in range(3)]
    sg_t = view(AR3, 24576, [128, 512], BF16)
    hq_t = view(AR3, 25600, [128, 512], BF16)
    ib_s = view(AR5, 0, [128, 1024], F32)
    of_s = view(AR5, 4096, [128, 1024], F32)
    at_t = view(AR5, 8192, [128, 1024], BF16)
    x_t = view(AR5, 10240, [128, 1024], F32)
    ih_row = view(AR5, 10240, [1, 1024], F32)

    def h_slice(mp, qoff, th_unused=None):
        base = h_lo if mp < 32 else h_hi
        return base[:, mp % 32, qoff:qoff + 512]

    for th in range(2):
        tsl = slice(th * 1024, (th + 1) * 1024)
        kb.wait("sync", "sT", T.get("xnth_free", 0))
        lxn = kb.dma("sync", xn_th[:, :, :],
                     s_xn.ap().transpose([1, 0, 2])[:, :, tsl], "sYL")
        for mp in range(64):
            bg, bu = gbuf[mp % 3], ubuf[mp % 3]
            kb.wait("sync", "sT", T.get("gumm%d" % (mp % 3), 0))
            kb.wait("gpsimd", "sT", T.get("gumm%d" % (mp % 3), 0))
            if th == 0 and mp == 0:
                kb.wait("sync", "sC", 2)
                kb.wait("gpsimd", "sC", 2)
            lg = kb.dma("sync", bg[:, :], q_out.ap()[mp], "sYL")
            lu = kb.dma("gpsimd", bu[:, :], q_out.ap()[64 + mp], "sG")
            kb.wait("tensor", "sYL", lg)
            kb.wait("tensor", "sG", lu)
            kb.wait("tensor", "sYL", lxn)
            for part2, wt, bb0 in (("u", bu, 2), ("g", bg, 0)):
                for bb in (bb0, bb0 + 1):
                    kb.wait("tensor", "sV", T.get("hfree%d" % bb, 0))
                    kb.wait("tensor", "sS", T.get("sfree%d" % bb, 0))
                for kc in range(16):
                    for half in range(2):
                        kb.mm(psA[:, (bb0 + half) * 512:(bb0 + half + 1) * 512],
                              wt[:, kc * 128:(kc + 1) * 128],
                              xn_th[:, kc, half * 512:half * 512 + 512],
                              kc == 0, kc == 15)
                T["gum_%s0" % part2] = kb.n["sT"]
                T["gum_%s1" % part2] = kb.n["sT"]
            for ih in range(2):
                bg_b, bu_b = ih, 2 + ih
                qoff = ih * 512
                kb.wait("scalar", "sT", T["gum_g%d" % ih])
                kb.wait("scalar", "sV", T.get("sg_free", 0))
                sgv = kb.emit("scalar", lambda e, b=bg_b: e.activation(
                    sg_t[:, :], psA[:, b * 512:(b + 1) * 512], AF.Silu), inc="sS")
                T["sfree%d" % bg_b] = sgv
                kb.wait("vector", "sS", sgv)
                kb.wait("vector", "sT", T["gum_u%d" % ih])
                hv = kb.emit("vector", lambda e, b=bu_b, d=h_slice(mp, qoff):
                             e.tensor_tensor(d, sg_t[:, :],
                                             psA[:, b * 512:(b + 1) * 512], ALU.mult),
                             inc="sV")
                T["hfree%d" % bu_b] = hv
                T["hfree%d" % bg_b] = hv
                T["sg_free"] = hv
                kb.wait("vector", "sT", T.get("hq_free", 0))
                hqv = kb.emit("vector", lambda e, d=h_slice(mp, qoff):
                              e.tensor_tensor(hq_t[:, :], d, d, ALU.mult), inc="sV")
                kb.wait("tensor", "sV", hqv)
                hsv = kb.mm(psB[0:1, qoff:qoff + 512], ones1_b[:, :], hq_t[:, :],
                            mp == 0, mp == 63)
                T["hq_free"] = hsv
            T["gumm%d" % (mp % 3)] = kb.n["sT"]
            if d_h is not None and th == 0:
                kb.wait("sync", "sV", kb.n["sV"])
                kb.dma("sync", d_h.ap()[mp], h_slice(mp, 0, None).tensor[
                    :, mp % 32, :] if False else h_lo[:, mp, :] if mp < 32
                    else h_hi[:, mp - 32, :], "sYS")
        kb.barrier()
        sv = kb.emit("scalar", lambda e: e.activation(
            ih_row[:, :], psB[0:1, 0:1024], AF.Sqrt, bias=C_H_BI, scale=C_H_SC),
            inc="sS")
        kb.wait("vector", "sS", sv)
        rv = kb.emit("vector", lambda e: e.reciprocal(ih_row[:, :], ih_row[:, :]),
                     inc="sV")
        kb.wait("tensor", "sV", rv)
        kb.mm(psB[:, 0:512], onesk1_f[:, :], ih_row[:, 0:512], True, True)
        b2 = kb.mm(psB[:, 512:1024], onesk1_f[:, :], ih_row[:, 512:1024], True, True)
        kb.wait("vector", "sT", b2)
        kb.emit("vector", lambda e: e.tensor_copy(ib_s[:, :], psB[:, 0:1024]), inc="sV")
        kb.barrier()
        wp_sw = view(AR3, 0, [128, 64, 256], BF16)
        for sw in range(16):
            par = sw % 2
            if par == 0:
                kb.wait("sync", "sT", T.get("sw_free1", 0))
                lw = kb.dma("sync", wp_sw[:, :, :],
                            q_out.ap()[128:192, :, (sw // 2) * 256:(sw // 2 + 1) * 256]
                            .transpose([1, 0, 2]), "sYL")
                T["wpsw_ld"] = lw
            kb.wait("tensor", "sYL", T["wpsw_ld"])
            kb.wait("tensor", "sV", T.get("of_done%d" % par, 0))
            for kc in range(64):
                for t2 in range(2):
                    kb.mm(psA[:, par * 1024 + t2 * 512: par * 1024 + (t2 + 1) * 512],
                          wp_sw[:, kc, par * 128:(par + 1) * 128],
                          h_slice(kc, t2 * 512), kc == 0, kc == 63)
            T["sw_free%d" % par] = kb.n["sT"]
            mg = sw
            kb.wait("sync", "sV", T.get("at_free", 0))
            la = kb.dma("sync", at_t[:, :], s_attn.ap()[mg][:, tsl], "sYL")
            lx = kb.dma("sync", x_t[:, :], xt.ap()[mg * 128:(mg + 1) * 128, tsl],
                        "sYL")
            kb.wait("vector", "sT", T["sw_free%d" % par])
            kb.wait("vector", "sYS", T.get("o_st", 0))
            kb.emit("vector", lambda e, p_=par: e.tensor_tensor(
                of_s[:, :], psA[:, p_ * 1024:(p_ + 1) * 1024], ib_s[:, :],
                ALU.mult), inc="sV")
            kb.wait("vector", "sYL", la)
            av2 = kb.emit("vector", lambda e: e.tensor_tensor(
                of_s[:, :], of_s[:, :], at_t[:, :], ALU.add), inc="sV")
            T["at_free"] = av2
            kb.wait("vector", "sYL", lx)
            ofv = kb.emit("vector", lambda e: e.tensor_tensor(
                of_s[:, :], of_s[:, :], x_t[:, :], ALU.add), inc="sV")
            T["of_done%d" % par] = ofv
            kb.wait("sync", "sV", ofv)
            ostv = kb.dma("sync", o.ap()[mg * 128:(mg + 1) * 128, tsl],
                          of_s[:, :], "sYS")
            T["o_st"] = ostv
            T["xnth_free"] = kb.n["sT"]
        kb.barrier()

    kb.wait("sync", "sYS", kb.n["sYS"])
    kb.emit("sync", lambda e: e.nop())
    kb.wait("gpsimd", "sYS", kb.n["sYS"])
    kb.emit("gpsimd", lambda e: e.nop())

    with es:
        sems = {s: es.enter_context(nc.semaphore(s)) for s in
                ("sG", "sYL", "sYS", "sT", "sV", "sS", "sC")}
        with nc.Block() as block:
            kb.replay(block, sems)
    return nc


def _repack_proj(w):
    cin, cout = w.shape
    r = np.asarray(w, np.float32).reshape(cin // 128, 128, cout // 128, 128)
    return np.ascontiguousarray(r.transpose(2, 1, 0, 3))


def _repack_gu(w):
    h, d = w.shape
    r = np.asarray(w, np.float32).reshape(h // 128, 128, d // 128, 128)
    return np.ascontiguousarray(r.transpose(0, 3, 2, 1))


def _repack_wp(w):
    dout, hin = w.shape
    r = np.asarray(w, np.float32).reshape(dout // 128, 128, hin // 128, 128)
    return np.ascontiguousarray(r.transpose(2, 3, 0, 1))


def _tables(T=4096, H=128):
    fraction = np.arange(0, H, 2, dtype=np.float64) / H
    timescale = 10000.0 ** fraction
    sinusoid = np.arange(T, dtype=np.float64)[:, None] / timescale[None, :]
    sinusoid = np.concatenate([sinusoid, sinusoid], axis=-1)
    return (np.ascontiguousarray(np.sin(sinusoid).T.astype(np.float32)),
            np.ascontiguousarray(np.cos(sinusoid).T.astype(np.float32)))


def kernel(L0_attn_x, L0_attn_Wq, L0_attn_Wk, L0_attn_Wv, L0_attn_Wout,
           L0_ffn_w_gate_up, L0_ffn_w_proj):
    x = np.asarray(L0_attn_x, dtype=np.float32)
    B, Tt, C = x.shape
    xt_full = np.ascontiguousarray(x.reshape(B * Tt, C).T)

    shared = {
        "wq": _repack_proj(L0_attn_Wq), "wk": _repack_proj(L0_attn_Wk),
        "wv": _repack_proj(L0_attn_Wv), "wo": _repack_proj(L0_attn_Wout),
        "ones1": np.ones((128, 1), np.float32),
        "onesk1": np.ones((1, 128), np.float32),
        "eye": np.eye(128, dtype=np.float32),
    }
    wgu_r = _repack_gu(L0_ffn_w_gate_up).reshape(128, 128, 2048)
    wp_r = _repack_wp(L0_ffn_w_proj).reshape(64, 128, 2048)
    wall = np.concatenate([wgu_r, wp_r], axis=0)
    wshards = [np.ascontiguousarray(wall[24 * c:24 * (c + 1)]) for c in range(8)]
    sel2 = np.zeros((2, 128), np.float32)
    sel2[0, :64] = 1.0
    sel2[1, 64:] = 1.0
    segc = np.zeros((128, 2), np.float32)
    segc[:64, 0] = 1.0
    segc[64:, 1] = 1.0
    rmat = np.zeros((128, 128), np.float32)
    for m in range(64):
        rmat[m + 64, m] = -1.0
        rmat[m, m + 64] = 1.0
    cvec = np.zeros((128, 16), np.float32)
    for i, c in enumerate(CONSTS):
        cvec[:, i] = c
    shared.update({"sel2": sel2, "segc": segc, "rmat": rmat, "cvec": cvec})
    sint_t, cost_t = _tables()

    nc = build(_DBG)
    in_maps = []
    for core in range(8):
        b, h = core // 2, core % 2
        base = b * Tt
        own = xt_full[:, base + h * 2048: base + (h + 1) * 2048]
        par = xt_full[:, base + (1 - h) * 2048: base + (2 - h) * 2048]
        pos = np.concatenate([np.arange(h * 2048, (h + 1) * 2048),
                              np.arange((1 - h) * 2048, (2 - h) * 2048)])
        im = dict(shared)
        im["wshard"] = wshards[core]
        im["xt"] = np.ascontiguousarray(np.concatenate([own, par], axis=1))
        im["cost"] = np.ascontiguousarray(cost_t[:, pos])
        im["sint"] = np.ascontiguousarray(sint_t[:, pos])
        in_maps.append(im)
    res = run_bass_kernel_spmd(nc, in_maps, core_ids=list(range(8)))
    out = np.empty((B, Tt, C), np.float32)
    for core in range(8):
        b, h = core // 2, core % 2
        out[b, h * 2048:(h + 1) * 2048, :] = res.results[core]["o"].T
    return out


# version 08
# speedup vs baseline: 1.0164x; 1.0164x over previous
import numpy as np
import ml_dtypes
from contextlib import ExitStack

import concourse.bass as bass
import concourse.mybir as mybir
from concourse.bass_utils import run_bass_kernel_spmd

bf16 = ml_dtypes.bfloat16
F32 = mybir.dt.float32
BF16 = mybir.dt.bfloat16
U8 = mybir.dt.uint8
AF = mybir.ActivationFunctionType
ALU = mybir.AluOpType

ENGINES = ("tensor", "vector", "scalar", "gpsimd", "sync")

C_LN_EPS = float(np.float32(1e-5))
C_EXP_BIAS = float(np.float32(-24.0))
C_Q_SC = float(np.float32(128.0 / 2048.0))
C_Q_BI = float(np.float32(128.0 * 1e-6))
C_K_SC = float(np.float32(1.0 / 512.0))
C_K_BI = float(np.float32(1e-6))
C_H_SC = float(np.float32(1.0 / 8192.0))
C_H_BI = float(np.float32(np.finfo(np.float32).eps))
C_Q64 = float(np.float32(1.0 / 64.0))
C_CLIP8 = float(np.asarray(1e-8, dtype=np.float32).astype(bf16).astype(np.float32))
CONSTS = [C_LN_EPS, C_EXP_BIAS, C_Q_SC, C_Q_BI, C_K_SC, C_K_BI, C_H_SC, C_H_BI, C_Q64]

_DBG = ()


class KB:

    def __init__(self):
        self.nc = bass.Bass("TRN2", target_bir_lowering=False)
        self.ops = {e: [] for e in ENGINES}
        self.n = {s: 0 for s in ("sG", "sYL", "sYS", "sT", "sV", "sS", "sC")}
        self.trk = {}
        self.pend = {e: [] for e in ENGINES}

    def wait(self, eng, sem, val):
        if val and val > 0:
            self.pend[eng].append((sem, val))

    def emit(self, eng, fn, inc=None, amt=1):
        waits, self.pend[eng] = self.pend[eng], []
        post = None
        if inc is not None:
            self.n[inc] += amt
            post = self.n[inc]
        self.ops[eng].append((waits, fn, inc, amt))
        return post

    def dma(self, queue, dst, src, sem):
        return self.emit(queue, lambda e, d=dst, s=src: e.dma_start(d, s), inc=sem, amt=16)

    def mm(self, out, lhsT, rhs, start, stop):
        import traceback
        site = traceback.extract_stack()[-2]
        def f(e, o=out, l=lhsT, r=rhs, a=start, b=stop, si=site):
            try:
                return e.matmul(o, l, r, start=a, stop=b)
            except Exception:
                print("MM FAIL at", si.filename, si.lineno, "lhsT", l.shape, "rhs", r.shape,
                      "out", o.shape)
                raise
        return self.emit("tensor", f, inc="sT")

    def barrier(self):
        snap = [(s, v) for s, v in self.n.items() if v > 0]
        for e in ENGINES:
            self.pend[e].extend(snap)

    def replay(self, block, sems):
        def mk(stream):
            def body(e):
                for waits, fn, inc, amt in stream:
                    for s, v in waits:
                        e.wait_ge(sems[s], v)
                    inst = fn(e)
                    if inc is not None:
                        if inst is None:
                            inst = e.nop()
                        inst.then_inc(sems[inc], amt)
            return body
        for name in ENGINES:
            if self.ops[name]:
                getattr(block, name)(mk(self.ops[name]))


def build(dbg=()):
    kb = KB()
    nc = kb.nc
    T = kb.trk

    xt = nc.dram_tensor("xt", [2048, 4096], F32, kind="ExternalInput")
    wq = nc.dram_tensor("wq", [16, 128, 16, 128], F32, kind="ExternalInput")
    wk = nc.dram_tensor("wk", [4, 128, 16, 128], F32, kind="ExternalInput")
    wv = nc.dram_tensor("wv", [4, 128, 16, 128], F32, kind="ExternalInput")
    wo = nc.dram_tensor("wo", [16, 128, 16, 128], F32, kind="ExternalInput")
    wshard = nc.dram_tensor("wshard", [24, 128, 2048], F32, kind="ExternalInput")
    cost = nc.dram_tensor("cost", [128, 4096], F32, kind="ExternalInput")
    sint = nc.dram_tensor("sint", [128, 4096], F32, kind="ExternalInput")
    ones1 = nc.dram_tensor("ones1", [128, 1], F32, kind="ExternalInput")
    onesk1 = nc.dram_tensor("onesk1", [1, 128], F32, kind="ExternalInput")
    sel2 = nc.dram_tensor("sel2", [2, 128], F32, kind="ExternalInput")
    segc = nc.dram_tensor("segc", [128, 2], F32, kind="ExternalInput")
    eye = nc.dram_tensor("eye", [128, 128], F32, kind="ExternalInput")
    rmat = nc.dram_tensor("rmat", [128, 128], F32, kind="ExternalInput")
    cvec = nc.dram_tensor("cvec", [128, 16], F32, kind="ExternalInput")
    o = nc.dram_tensor("o", [2048, 2048], F32, kind="ExternalOutput")

    def scratch(name, shape, dt_, flag):
        return nc.dram_tensor(name, shape, dt_,
                              kind="ExternalOutput" if flag in dbg else "Internal")
    s_xn = scratch("s_xn", [16, 128, 2048], BF16, "xn")
    s_enc = scratch("s_enc", [16, 128, 2048], BF16, "enc")
    s_attn = scratch("s_attn", [16, 128, 2048], BF16, "attn")
    q_in = nc.dram_tensor("q_in", [24, 128, 2048], BF16)
    q_out = nc.dram_tensor("q_out", [192, 128, 2048], BF16, addr_space="Shared")
    bar_in = nc.dram_tensor("bar_in", [1, 128], F32)
    bar_out = nc.dram_tensor("bar_out", [1, 128], F32, addr_space="Shared")
    d_qo = (nc.dram_tensor("d_qo", [192, 128, 2048], BF16, kind="ExternalOutput")
            if "wgu" in dbg else None)
    d_kqv = None
    if "kqv" in dbg:
        d_kqv = (nc.dram_tensor("d_kp", [4, 128, 4096], BF16, kind="ExternalOutput"),
                 nc.dram_tensor("d_qp", [16, 128, 2048], BF16, kind="ExternalOutput"),
                 nc.dram_tensor("d_vt", [32, 128, 512], BF16, kind="ExternalOutput"))
    d_h = (nc.dram_tensor("d_h", [64, 128, 1024], BF16, kind="ExternalOutput")
           if "h" in dbg else None)

    es = ExitStack()
    sbt = lambda n_, s_, d_: es.enter_context(nc.sbuf_tensor(n_, s_, d_))

    AR1 = sbt("AR1", [128, 65536], U8)
    AR2 = sbt("AR2", [128, 65536], U8)
    AR3 = sbt("AR3", [128, 32768], U8)
    AR4 = sbt("AR4", [128, 32768], U8)
    AR5 = sbt("AR5", [128, 14336], U8)

    def view(ar, off, shape, dt_):
        nbytes = int(np.prod(shape[1:])) * mybir.dt.size(dt_)
        assert off % 32 == 0 and off + nbytes <= int(ar.shape[1]), (off, nbytes, ar.name)
        flat = ar.reshape([128, int(ar.shape[1])]).bitcast(dt_)
        elem = mybir.dt.size(dt_)
        ap = flat[0:shape[0], off // elem: off // elem + nbytes // elem]
        if len(shape) == 3:
            ap = ap.rearrange("p (a b) -> p a b", a=shape[1], b=shape[2])
        return ap

    ones1_b = sbt("ones1_b", [128, 1], BF16)
    onesk1_f = sbt("onesk1_f", [1, 128], F32)
    sel2_b = sbt("sel2_b", [2, 128], BF16)
    segc_b = sbt("segc_b", [128, 2], BF16)
    eye_b = sbt("eye_b", [128, 128], BF16)
    rmat_b = sbt("rmat_b", [128, 128], BF16)
    cv = sbt("cv", [128, 16], F32)
    for i, c in enumerate(CONSTS):
        nc.const_aps.register(F32, c, cv[:, i:i + 1])

    psA = es.enter_context(nc.psum_tensor("psA", [128, 2048], F32))
    psB = es.enter_context(nc.psum_tensor("psB", [128, 1536], F32))
    psT = es.enter_context(nc.psum_tensor("psT", [128, 1024], BF16))

    for dst, src in ((ones1_b, ones1), (onesk1_f, onesk1), (sel2_b, sel2),
                     (segc_b, segc), (eye_b, eye), (rmat_b, rmat),
                     (cv, cvec)):
        kb.dma("gpsimd", dst[:, :], src.ap(), "sG")
    kb.barrier()

    xt_r = xt.ap().rearrange("(kc p) t -> p kc t", p=128)

    kp = view(AR3, 0, [128, 4, 4096], BF16)
    vt = view(AR4, 0, [128, 32, 512], BF16)
    qp = view(AR2, 0, [128, 16, 2048], BF16)
    xnh = view(AR1, 0, [128, 16, 2048], BF16)
    wbuf = [view(AR5, 0, [128, 2048], BF16), view(AR5, 4096, [128, 2048], BF16)]
    sqt = [view(AR5, 8192, [128, 512], BF16), view(AR5, 9216, [128, 512], BF16)]
    m_row = view(AR5, 10240, [1, 512], F32)
    a_row = view(AR5, 12288, [1, 512], F32)
    cos_sl = view(AR5, 0, [128, 512], F32)
    sin_sl = view(AR5, 2048, [128, 512], F32)
    se_t = view(AR5, 4096, [128, 512], BF16)
    t1_t = view(AR5, 5120, [128, 512], BF16)
    t2_t = view(AR5, 6144, [128, 512], BF16)
    rope_sq = view(AR5, 7168, [128, 512], BF16)

    qbuf = [view(AR2, 51200, [128, 2048], BF16), view(AR2, 55296, [128, 2048], BF16)]
    r_tQ = view(AR2, 59392, [128, 512], BF16)
    sc_bfQ = [view(AR2, 60416, [2, 512], BF16), view(AR2, 61440, [2, 512], BF16)]
    scB = view(AR2, 62464, [128, 512], BF16)
    rcp128 = view(AR2, 63488, [128, 512], F32)

    QPAR = [0]

    def quant_tile(w_ap):
        qq = QPAR[0]
        QPAR[0] ^= 1
        sc_t = sc_bfQ[qq]
        pst = psA[:, 1024 + qq * 512: 1536 + qq * 512]
        kb.wait("scalar", "sV", T.get("q_done", 0))
        av = kb.emit("scalar", lambda e, w=w_ap: e.activation(
            r_tQ[:, :], w, AF.Abs), inc="sS")
        kb.wait("tensor", "sS", av)
        kb.wait("tensor", "sS", T.get("q_segfree", 0))
        sv = kb.mm(psB[0:2, 1024:1536], segc_b[:, :], r_tQ[:, :], True, True)
        kb.wait("scalar", "sT", sv)
        kb.wait("scalar", "sT", T.get("q_bc%d" % qq, 0))
        scv = kb.emit("scalar", lambda e, d=sc_t: e.activation(
            d[:, :], psB[0:2, 1024:1536], AF.Copy, scale=C_Q64), inc="sS")
        T["q_segfree"] = scv
        kb.wait("tensor", "sS", scv)
        kb.wait("tensor", "sV", T.get("q_pst%d" % qq, 0))
        bsv = kb.mm(pst, sel2_b[:, :], sc_t[:, :], True, True)
        T["q_bc%d" % qq] = bsv
        kb.wait("vector", "sT", bsv)
        cmv = kb.emit("vector", lambda e, p=pst: e.tensor_scalar(
            scB[:, :], p, C_CLIP8, None, ALU.max), inc="sV")
        T["q_pst%d" % qq] = cmv
        kb.emit("vector", lambda e: e.reciprocal(rcp128[:, :], scB[:, :]), inc="sV")
        kb.emit("vector", lambda e, w=w_ap: e.tensor_tensor(
            r_tQ[:, :], w, rcp128[:, :], ALU.mult), inc="sV")
        kb.emit("vector", lambda e: e.tensor_scalar(
            r_tQ[:, :], r_tQ[:, :], 192.0, None, ALU.add), inc="sV")
        kb.emit("vector", lambda e: e.tensor_scalar(
            r_tQ[:, :], r_tQ[:, :], -192.0, None, ALU.add), inc="sV")
        kb.emit("vector", lambda e: e.tensor_scalar(
            r_tQ[:, :], r_tQ[:, :], -1.0, 1.0, ALU.max, ALU.min), inc="sV")
        wv_ = kb.emit("vector", lambda e, w=w_ap: e.tensor_tensor(
            w, r_tQ[:, :], scB[:, :], ALU.mult), inc="sV")
        T["q_done"] = wv_
        return wv_

    def p1q_block(bi):
        buf = qbuf[bi % 2]
        kb.wait("gpsimd", "sYS", T.get("p1q_st%d" % (bi % 2), 0))
        ld = kb.dma("gpsimd", buf[:, :], wshard.ap()[bi], "sG")
        kb.wait("scalar", "sG", ld)
        qv = None
        for qq in range(4):
            qv = quant_tile(buf[:, qq * 512:(qq + 1) * 512])
        kb.wait("sync", "sV", qv)
        st = kb.dma("sync", q_in.ap()[bi], buf[:, :], "sYS")
        T["p1q_st%d" % (bi % 2)] = st
        T["p1q_store_last"] = st
        T["p1q_V"] = kb.n["sV"]
        T["p1q_T"] = kb.n["sT"]

    P1Q_SCHED = {ji: [ji] for ji in range(18)}
    P1Q_SCHED[18] = [18, 19]
    P1Q_SCHED[19] = [20, 21]
    P1Q_SCHED[20] = [22, 23]

    def ln_half(sh):
        for tb in range(4):
            t0 = sh * 2048 + tb * 512
            tsl = slice(tb * 512, (tb + 1) * 512)
            ldv = kb.dma("gpsimd", xnh[:, :, tsl], xt_r[:, :, t0:t0 + 512], "sG")
            kb.wait("tensor", "sG", ldv)
            kb.wait("tensor", "sV", T.get("ln_psum_free", 0))
            kb.wait("tensor", "sV", T.get("ln_apply", 0))
            sxv = None
            for kc in range(16):
                sxv = kb.mm(psB[0:1, 0:512], ones1_b[:, :], xnh[:, kc, tsl],
                            kc == 0, kc == 15)
            sxxv = None
            for kc in range(16):
                kb.wait("vector", "sG", ldv)
                kb.wait("vector", "sT", T.get("lnsq%d" % (kc % 2), 0))
                sqv = kb.emit("vector",
                              lambda e, a=xnh[:, kc, tsl], q=sqt[kc % 2][:, :]:
                              e.tensor_tensor(q, a, a, ALU.mult), inc="sV")
                kb.wait("tensor", "sV", sqv)
                sxxv = kb.mm(psB[0:1, 512:1024], ones1_b[:, :], sqt[kc % 2][:, :],
                             kc == 0, kc == 15)
                T["lnsq%d" % (kc % 2)] = sxxv
            mtv = kb.emit("vector", lambda e: e.tensor_scalar(
                m_row[:, :], psB[0:1, 0:512], 1.0 / 2048.0, None, ALU.mult), inc="sV")
            kb.wait("vector", "sT", sxxv)
            kb.emit("vector", lambda e: e.tensor_scalar(
                a_row[:, :], psB[0:1, 512:1024], 1.0 / 2048.0, None, ALU.mult), inc="sV")
            kb.emit("vector", lambda e: e.tensor_tensor(
                psB[0:1, 512:1024], m_row[:, :], m_row[:, :], ALU.mult), inc="sV")
            vv = kb.emit("vector", lambda e: e.tensor_tensor(
                a_row[:, :], a_row[:, :], psB[0:1, 512:1024], ALU.subtract), inc="sV")
            T["ln_psum_free"] = vv
            kb.wait("scalar", "sV", vv)
            sv = kb.emit("scalar", lambda e: e.activation(
                a_row[:, :], a_row[:, :], AF.Sqrt, bias=C_LN_EPS), inc="sS")
            kb.wait("tensor", "sV", mtv)
            kb.mm(psA[:, 1024:1536], onesk1_f[:, :], m_row[:, :], True, True)
            kb.wait("tensor", "sS", sv)
            biv = kb.mm(psA[:, 1536:2048], onesk1_f[:, :], a_row[:, :], True, True)
            kb.wait("vector", "sT", biv)
            rv = kb.emit("vector", lambda e: e.reciprocal(
                psA[:, 1536:2048], psA[:, 1536:2048]), inc="sV")
            lastv = None
            for kc in range(16):
                sl = xnh[:, kc, tsl]
                kb.emit("vector", lambda e, a=sl: e.tensor_tensor(
                    a, a, psA[:, 1024:1536], ALU.subtract), inc="sV")
                lastv = kb.emit("vector", lambda e, a=sl: e.tensor_tensor(
                    a, a, psA[:, 1536:2048], ALU.mult), inc="sV")
            T["ln_apply"] = lastv
            if sh == 0:
                kb.wait("sync", "sV", lastv)
                kb.dma("sync", s_xn.ap().transpose([1, 0, 2])[:, :, tsl],
                       xnh[:, :, tsl], "sYS")

    def proj_half(sh):
        jobs = [("k", m) for m in range(4)] + [("v", m) for m in range(4)]
        if sh == 0:
            jobs += [("q", m) for m in range(16)]
        for ji, (kind, m) in enumerate(jobs):
            if sh == 0:
                for bi in P1Q_SCHED.get(ji, []):
                    p1q_block(bi)
            buf = wbuf[ji % 2]
            src = {"k": wk, "v": wv, "q": wq}[kind]
            kb.wait("gpsimd", "sT", T.get("wbuf%d" % (ji % 2), 0))
            ldv = kb.dma("gpsimd", buf[:, :],
                         src.ap()[m].rearrange("p a b -> p (a b)"), "sG")
            banks = [psA[:, 0:512], psA[:, 512:1024], psB[:, 0:512], psB[:, 512:1024]]
            kb.wait("tensor", "sG", ldv)
            for pb_i in range(4):
                pv = T.get("pevac%d" % pb_i)
                if pv is not None:
                    kb.wait("tensor", pv[1], pv[0])
            for kc in range(16):
                for tb in range(4):
                    mmv = kb.mm(banks[tb], buf[:, kc * 128:(kc + 1) * 128],
                                xnh[:, kc, tb * 512:(tb + 1) * 512],
                                kc == 0, kc == 15)
            for tb in range(4):
                tsl = slice(tb * 512, (tb + 1) * 512)
                pb_i = tb
                pbank = banks[tb]
                if kind == "k":
                    dst = kp[:, m, sh * 2048 + tb * 512: sh * 2048 + (tb + 1) * 512]
                    eng = "scalar" if pb_i % 2 == 0 else "vector"
                    sem = "sS" if pb_i % 2 == 0 else "sV"
                    kb.wait(eng, "sT", mmv)
                    if eng == "scalar":
                        ev = (kb.emit(eng, lambda e, d=dst, p=pbank:
                                      e.activation(d, p, AF.Copy), inc=sem), sem)
                    else:
                        ev = (kb.emit(eng, lambda e, d=dst, p=pbank:
                                      e.tensor_copy(d, p), inc=sem), sem)
                elif kind == "q":
                    dst = qp[:, m, tsl]
                    eng = "scalar" if pb_i % 2 == 0 else "vector"
                    sem = "sS" if pb_i % 2 == 0 else "sV"
                    kb.wait(eng, "sT", mmv)
                    if m >= 12:
                        kb.wait(eng, "sYS", T.get("p1q_store_last", 0))
                        kb.wait(eng, "sV", T.get("p1q_V", 0))
                        kb.wait(eng, "sT", T.get("p1q_T", 0))
                    if eng == "scalar":
                        ev = (kb.emit(eng, lambda e, d=dst, p=pbank:
                                      e.activation(d, p, AF.Copy), inc=sem), sem)
                    else:
                        ev = (kb.emit(eng, lambda e, d=dst, p=pbank:
                                      e.tensor_copy(d, p), inc=sem), sem)
                else:
                    vtile = sqt[tb % 2]
                    kb.wait("scalar", "sT", mmv)
                    kb.wait("scalar", "sV", T.get("vtile%d" % (tb % 2), 0))
                    ev0 = kb.emit("scalar", lambda e, d=vtile[:, :], p=pbank:
                                  e.activation(d, p, AF.Copy), inc="sS")
                    ev = (ev0, "sS")
                    kb.wait("tensor", "sS", ev0)
                    kb.wait("tensor", "sV", T.get("psT_free", 0))
                    tv = None
                    for jj in range(4):
                        tv = kb.emit("tensor", lambda e,
                                     oo=psT[:, jj * 128:(jj + 1) * 128],
                                     ii=vtile[:, jj * 128:(jj + 1) * 128]:
                                     e.transpose(oo, ii, eye_b[:, :]), inc="sT")
                    dst = vt[:, sh * 16 + tb * 4: sh * 16 + (tb + 1) * 4,
                             m * 128:(m + 1) * 128]
                    kb.wait("vector", "sT", tv)
                    cpv = kb.emit("vector", lambda e, d=dst,
                                  s=psT[:, 0:512].rearrange("p (a b) -> p a b", a=4, b=128):
                                  e.tensor_copy(d, s), inc="sV")
                    T["psT_free"] = cpv
                    T["vtile%d" % (tb % 2)] = cpv
                T["pevac%d" % pb_i] = ev
            T["wbuf%d" % (ji % 2)] = kb.n["sT"]

    def rope(kind):
        nch = 4 if kind == "k" else 16
        buf3 = kp if kind == "k" else qp
        sc, bi = (C_K_SC, C_K_BI) if kind == "k" else (C_Q_SC, C_Q_BI)
        for tb in range(8 if kind == "k" else 4):
            tsl = slice(tb * 512, (tb + 1) * 512)
            kb.wait("sync", "sV", T.get("rope_cs", 0))
            kb.dma("sync", cos_sl[:, :], cost.ap()[:, tsl], "sYL")
            ls = kb.dma("sync", sin_sl[:, :], sint.ap()[:, tsl], "sYL")
            smv = None
            for ch in range(nch):
                kb.wait("vector", "sT", T.get("rope_sq", 0))
                qv = kb.emit("vector", lambda e, a=buf3[:, ch, tsl], q=rope_sq[:, :]:
                             e.tensor_tensor(q, a, a, ALU.mult), inc="sV")
                kb.wait("tensor", "sV", qv)
                smv = kb.mm(psB[0:1, 0:512], ones1_b[:, :], rope_sq[:, :],
                            ch == 0, ch == nch - 1)
                T["rope_sq"] = smv
            kb.wait("scalar", "sT", smv)
            sv = kb.emit("scalar", lambda e, s=sc, b=bi: e.activation(
                m_row[:, :], psB[0:1, 0:512], AF.Sqrt, bias=b, scale=s), inc="sS")
            kb.wait("vector", "sS", sv)
            rv = kb.emit("vector", lambda e: e.reciprocal(m_row[:, :], m_row[:, :]),
                         inc="sV")
            kb.wait("tensor", "sV", rv)
            kb.wait("tensor", "sS", sv)
            bv = kb.mm(psA[:, 1024:1536], onesk1_f[:, :], m_row[:, :], True, True)
            kb.wait("vector", "sT", bv)
            kb.wait("vector", "sYL", ls)
            cev = kb.emit("vector", lambda e: e.tensor_tensor(
                psB[:, 512:1024], cos_sl[:, :], psA[:, 1024:1536], ALU.mult), inc="sV")
            sev = kb.emit("vector", lambda e: e.tensor_tensor(
                se_t[:, :], sin_sl[:, :], psA[:, 1024:1536], ALU.mult), inc="sV")
            T["rope_cs"] = sev
            for ch in range(nch):
                sl = buf3[:, ch, tsl]
                kb.wait("tensor", "sV", T.get("rope_w", 0))
                rotv = kb.mm(psA[:, (ch % 2) * 512:(ch % 2) * 512 + 512],
                             rmat_b[:, :], sl, True, True)
                kb.wait("vector", "sV", cev)
                kb.emit("vector", lambda e, a=sl: e.tensor_tensor(
                    t1_t[:, :], a, psB[:, 512:1024], ALU.mult), inc="sV")
                kb.wait("vector", "sT", rotv)
                kb.emit("vector", lambda e, c=ch: e.tensor_tensor(
                    t2_t[:, :], psA[:, (c % 2) * 512:(c % 2) * 512 + 512],
                    se_t[:, :], ALU.mult), inc="sV")
                wv_ = kb.emit("vector", lambda e, a=sl: e.tensor_tensor(
                    a, t1_t[:, :], t2_t[:, :], ALU.add), inc="sV")
                T["rope_w"] = wv_

    for sh in (0, 1):
        ln_half(sh)
        kb.barrier()
        proj_half(sh)
        kb.barrier()
    kb.wait("gpsimd", "sYS", T["p1q_store_last"])
    kb.emit("gpsimd", lambda e: e.collective_compute(
        "AllGather", ALU.bypass, replica_groups=[list(range(8))],
        ins=[q_in.ap().opt()], outs=[q_out.ap().opt()]), inc="sC")
    kb.emit("gpsimd", lambda e: e.collective_compute(
        "AllReduce", ALU.add, replica_groups=[list(range(8))],
        ins=[bar_in.ap().opt()], outs=[bar_out.ap().opt()]), inc="sC")
    if d_qo is not None:
        kb.wait("gpsimd", "sC", 2)
        for i_ in range(6):
            kb.dma("gpsimd", d_qo.ap()[i_ * 32:(i_ + 1) * 32],
                   q_out.ap()[i_ * 32:(i_ + 1) * 32], "sG")
    rope("k")
    rope("q")
    kb.barrier()

    if d_kqv is not None:
        d_kp, d_qp, d_vt = d_kqv
        for m in range(4):
            kb.dma("sync", d_kp.ap()[m], kp[:, m, :], "sYS")
        for m in range(16):
            kb.dma("sync", d_qp.ap()[m], qp[:, m, :], "sYS")
        for j in range(32):
            kb.dma("sync", d_vt.ap()[j], vt[:, j, :], "sYS")
        kb.barrier()

    E_s = view(AR1, 0, [128, 32, 512], BF16)
    enc_qb = view(AR1, 32768, [128, 16, 512], BF16)
    rb_s = view(AR5, 0, [128, 512], F32)
    rs_row = view(AR5, 2048, [1, 512], F32)

    def pv_pair(g, n, g_kv):
        kb.wait("tensor", "sS", T["exp%d" % g])
        if g == 0:
            kb.wait("tensor", "sV", T.get("enc_done", 0))
        for jj in range(2):
            j = 2 * g + jj
            kb.mm(psB[:, 0:512], vt[:, j, g_kv * 128:(g_kv + 1) * 128],
                  E_s[:, j, :], j == 0, j == 31)

    pending_norm = []

    def flush_norm():
        while pending_norm:
            fn = pending_norm.pop(0)
            fn()

    for qb in range(4):
        qs = slice(qb * 512, (qb + 1) * 512)
        for n in range(16):
            g_kv = n // 4
            prev_rsA = T.get("rs_endA", 0)
            prev_rsB = T.get("rs_endB", 0)
            for g in range(16):
                grp = psA[:, (g % 2) * 1024:(g % 2) * 1024 + 1024]
                kb.wait("tensor", "sS", T.get("qkgrp%d" % (g % 2), 0))
                for jj in range(2):
                    j = 2 * g + jj
                    kb.mm(grp[:, jj * 512:(jj + 1) * 512],
                          kp[:, g_kv, j * 128:(j + 1) * 128], qp[:, n, qs],
                          True, True)
                if g == 0:
                    flush_norm()
                kb.wait("scalar", "sT", kb.n["sT"])
                kb.wait("scalar", "sT", prev_rsA if g < 8 else prev_rsB)
                ev = kb.emit("scalar", lambda e, g_=g, gr=grp: e.activation(
                    E_s[:, 2 * g_:2 * g_ + 2, :].rearrange("p a b -> p (a b)"),
                    gr, AF.Exp, bias=C_EXP_BIAS), inc="sS")
                T["exp%d" % g] = ev
                T["qkgrp%d" % (g % 2)] = ev
                if g >= 1:
                    pv_pair(g - 1, n, g_kv)
            pv_pair(15, n, g_kv)
            kb.wait("tensor", "sV", T.get("rs_done", 0))
            for j in range(16):
                kb.mm(psB[0:1, 512:1024], ones1_b[:, :], E_s[:, j, :],
                      j == 0, False)
            T["rs_endA"] = kb.n["sT"]
            for j in range(16, 32):
                kb.mm(psB[0:1, 512:1024], ones1_b[:, :], E_s[:, j, :],
                      False, j == 31)
            T["rs_endB"] = kb.n["sT"]
            kb.wait("vector", "sT", kb.n["sT"])
            rv = kb.emit("vector", lambda e: e.tensor_copy(
                rs_row[:, :], psB[0:1, 512:1024]), inc="sV")
            T["rs_done"] = rv

            def mk_norm(n_=n, rv_=rv):
                def do():
                    kb.wait("tensor", "sV", rv_)
                    bv = kb.mm(psB[:, 1024:1536], onesk1_f[:, :], rs_row[:, :],
                               True, True)
                    kb.wait("vector", "sT", bv)
                    kb.emit("vector", lambda e: e.reciprocal(
                        psB[:, 1024:1536], psB[:, 1024:1536]), inc="sV")
                    kb.emit("vector", lambda e: e.tensor_copy(
                        rb_s[:, :], psB[:, 1024:1536]), inc="sV")
                    env = kb.emit("vector", lambda e: e.tensor_tensor(
                        enc_qb[:, n_, :], psB[:, 0:512], rb_s[:, :], ALU.mult),
                        inc="sV")
                    T["enc_done"] = env
                return do
            pending_norm.append(mk_norm())
        flush_norm()
        kb.wait("sync", "sV", kb.n["sV"])
        kb.dma("sync", s_enc.ap().transpose([1, 0, 2])[:, :, qs],
               enc_qb[:, :, :], "sYS")
    kb.barrier()

    enc_full = view(AR1, 0, [128, 16, 2048], BF16)
    wobuf = [view(AR3, 0, [128, 2048], BF16), view(AR3, 4096, [128, 2048], BF16)]
    attn_t = [view(AR5, 8192, [128, 512], BF16), view(AR5, 9216, [128, 512], BF16)]
    lenc = kb.dma("sync", enc_full[:, :, :], s_enc.ap().transpose([1, 0, 2]), "sYL")
    for m in range(16):
        kb.wait("gpsimd", "sT", T.get("wobuf%d" % (m % 2), 0))
        ldv = kb.dma("gpsimd", wobuf[m % 2][:, :],
                     wo.ap()[m].rearrange("p a b -> p (a b)"), "sG")
        banks5 = [psA[:, 0:512], psA[:, 512:1024], psA[:, 1024:1536], psA[:, 1536:2048]]
        kb.wait("tensor", "sG", ldv)
        kb.wait("tensor", "sYL", lenc)
        for qb in range(4):
            kb.wait("tensor", "sS", T.get("aevac%d" % qb, 0))
        mmv = None
        for kc in range(16):
            for qb in range(4):
                mmv = kb.mm(banks5[qb], wobuf[m % 2][:, kc * 128:(kc + 1) * 128],
                            enc_full[:, kc, qb * 512:(qb + 1) * 512], kc == 0, kc == 15)
        for qb in range(4):
            kb.wait("scalar", "sT", mmv)
            kb.wait("scalar", "sYS", T.get("ast%d" % (qb % 2), 0))
            ev = kb.emit("scalar", lambda e, d=attn_t[qb % 2][:, :], p=banks5[qb]:
                         e.activation(d, p, AF.Copy), inc="sS")
            T["aevac%d" % qb] = ev
            kb.wait("sync", "sS", ev)
            stv = kb.dma("sync", s_attn.ap()[m][:, qb * 512:(qb + 1) * 512],
                         attn_t[qb % 2][:, :], "sYS")
            T["ast%d" % (qb % 2)] = stv
        T["wobuf%d" % (m % 2)] = kb.n["sT"]
    kb.barrier()

    h_lo = view(AR1, 0, [128, 32, 1024], BF16)
    h_hi = view(AR2, 0, [128, 32, 1024], BF16)
    xn_th = view(AR4, 0, [128, 16, 1024], BF16)
    gbuf = [view(AR3, i * 4096, [128, 2048], BF16) for i in range(3)]
    ubuf = [view(AR3, 12288 + i * 4096, [128, 2048], BF16) for i in range(3)]
    sg_t = view(AR3, 24576, [128, 512], BF16)
    hq_t = view(AR3, 25600, [128, 512], BF16)
    ib_s = view(AR5, 0, [128, 1024], F32)
    of_s = view(AR5, 4096, [128, 1024], F32)
    at_t = view(AR5, 8192, [128, 1024], BF16)
    x_t = view(AR5, 10240, [128, 1024], F32)
    ih_row = view(AR5, 10240, [1, 1024], F32)

    def h_slice(mp, qoff, th_unused=None):
        base = h_lo if mp < 32 else h_hi
        return base[:, mp % 32, qoff:qoff + 512]

    for th in range(2):
        tsl = slice(th * 1024, (th + 1) * 1024)
        kb.wait("sync", "sT", T.get("xnth_free", 0))
        lxn = kb.dma("sync", xn_th[:, :, :],
                     s_xn.ap().transpose([1, 0, 2])[:, :, tsl], "sYL")
        for mp in range(64):
            bg, bu = gbuf[mp % 3], ubuf[mp % 3]
            kb.wait("sync", "sT", T.get("gumm%d" % (mp % 3), 0))
            kb.wait("gpsimd", "sT", T.get("gumm%d" % (mp % 3), 0))
            if th == 0 and mp == 0:
                kb.wait("sync", "sC", 2)
                kb.wait("gpsimd", "sC", 2)
            lg = kb.dma("sync", bg[:, :], q_out.ap()[mp], "sYL")
            lu = kb.dma("gpsimd", bu[:, :], q_out.ap()[64 + mp], "sG")
            kb.wait("tensor", "sYL", lg)
            kb.wait("tensor", "sG", lu)
            kb.wait("tensor", "sYL", lxn)
            for part2, wt, bb0 in (("u", bu, 2), ("g", bg, 0)):
                for bb in (bb0, bb0 + 1):
                    kb.wait("tensor", "sV", T.get("hfree%d" % bb, 0))
                    kb.wait("tensor", "sS", T.get("sfree%d" % bb, 0))
                for kc in range(16):
                    for half in range(2):
                        kb.mm(psA[:, (bb0 + half) * 512:(bb0 + half + 1) * 512],
                              wt[:, kc * 128:(kc + 1) * 128],
                              xn_th[:, kc, half * 512:half * 512 + 512],
                              kc == 0, kc == 15)
                T["gum_%s0" % part2] = kb.n["sT"]
                T["gum_%s1" % part2] = kb.n["sT"]
            for ih in range(2):
                bg_b, bu_b = ih, 2 + ih
                qoff = ih * 512
                kb.wait("scalar", "sT", T["gum_g%d" % ih])
                kb.wait("scalar", "sV", T.get("sg_free", 0))
                sgv = kb.emit("scalar", lambda e, b=bg_b: e.activation(
                    sg_t[:, :], psA[:, b * 512:(b + 1) * 512], AF.Silu), inc="sS")
                T["sfree%d" % bg_b] = sgv
                kb.wait("vector", "sS", sgv)
                kb.wait("vector", "sT", T["gum_u%d" % ih])
                hv = kb.emit("vector", lambda e, b=bu_b, d=h_slice(mp, qoff):
                             e.tensor_tensor(d, sg_t[:, :],
                                             psA[:, b * 512:(b + 1) * 512], ALU.mult),
                             inc="sV")
                T["hfree%d" % bu_b] = hv
                T["hfree%d" % bg_b] = hv
                T["sg_free"] = hv
                kb.wait("vector", "sT", T.get("hq_free", 0))
                hqv = kb.emit("vector", lambda e, d=h_slice(mp, qoff):
                              e.tensor_tensor(hq_t[:, :], d, d, ALU.mult), inc="sV")
                kb.wait("tensor", "sV", hqv)
                hsv = kb.mm(psB[0:1, qoff:qoff + 512], ones1_b[:, :], hq_t[:, :],
                            mp == 0, mp == 63)
                T["hq_free"] = hsv
            T["gumm%d" % (mp % 3)] = kb.n["sT"]
            if d_h is not None and th == 0:
                kb.wait("sync", "sV", kb.n["sV"])
                kb.dma("sync", d_h.ap()[mp], h_slice(mp, 0, None).tensor[
                    :, mp % 32, :] if False else h_lo[:, mp, :] if mp < 32
                    else h_hi[:, mp - 32, :], "sYS")
        kb.barrier()
        sv = kb.emit("scalar", lambda e: e.activation(
            ih_row[:, :], psB[0:1, 0:1024], AF.Sqrt, bias=C_H_BI, scale=C_H_SC),
            inc="sS")
        kb.wait("vector", "sS", sv)
        rv = kb.emit("vector", lambda e: e.reciprocal(ih_row[:, :], ih_row[:, :]),
                     inc="sV")
        kb.wait("tensor", "sV", rv)
        kb.mm(psB[:, 0:512], onesk1_f[:, :], ih_row[:, 0:512], True, True)
        b2 = kb.mm(psB[:, 512:1024], onesk1_f[:, :], ih_row[:, 512:1024], True, True)
        kb.wait("vector", "sT", b2)
        kb.emit("vector", lambda e: e.tensor_copy(ib_s[:, :], psB[:, 0:1024]), inc="sV")
        kb.barrier()
        wp_sw = view(AR3, 0, [128, 64, 256], BF16)
        for sw in range(16):
            par = sw % 2
            if par == 0:
                kb.wait("sync", "sT", T.get("sw_free1", 0))
                lw = kb.dma("sync", wp_sw[:, :, :],
                            q_out.ap()[128:192, :, (sw // 2) * 256:(sw // 2 + 1) * 256]
                            .transpose([1, 0, 2]), "sYL")
                T["wpsw_ld"] = lw
            kb.wait("tensor", "sYL", T["wpsw_ld"])
            kb.wait("tensor", "sV", T.get("of_done%d" % par, 0))
            for kc in range(64):
                for t2 in range(2):
                    kb.mm(psA[:, par * 1024 + t2 * 512: par * 1024 + (t2 + 1) * 512],
                          wp_sw[:, kc, par * 128:(par + 1) * 128],
                          h_slice(kc, t2 * 512), kc == 0, kc == 63)
            T["sw_free%d" % par] = kb.n["sT"]
            mg = sw
            kb.wait("sync", "sV", T.get("at_free", 0))
            la = kb.dma("sync", at_t[:, :], s_attn.ap()[mg][:, tsl], "sYL")
            lx = kb.dma("sync", x_t[:, :], xt.ap()[mg * 128:(mg + 1) * 128, tsl],
                        "sYL")
            kb.wait("vector", "sT", T["sw_free%d" % par])
            kb.wait("vector", "sYS", T.get("o_st", 0))
            kb.emit("vector", lambda e, p_=par: e.tensor_tensor(
                of_s[:, :], psA[:, p_ * 1024:(p_ + 1) * 1024], ib_s[:, :],
                ALU.mult), inc="sV")
            kb.wait("vector", "sYL", la)
            av2 = kb.emit("vector", lambda e: e.tensor_tensor(
                of_s[:, :], of_s[:, :], at_t[:, :], ALU.add), inc="sV")
            T["at_free"] = av2
            kb.wait("vector", "sYL", lx)
            ofv = kb.emit("vector", lambda e: e.tensor_tensor(
                of_s[:, :], of_s[:, :], x_t[:, :], ALU.add), inc="sV")
            T["of_done%d" % par] = ofv
            kb.wait("sync", "sV", ofv)
            ostv = kb.dma("sync", o.ap()[mg * 128:(mg + 1) * 128, tsl],
                          of_s[:, :], "sYS")
            T["o_st"] = ostv
            T["xnth_free"] = kb.n["sT"]
        kb.barrier()

    kb.wait("sync", "sYS", kb.n["sYS"])
    kb.emit("sync", lambda e: e.nop())
    kb.wait("gpsimd", "sYS", kb.n["sYS"])
    kb.emit("gpsimd", lambda e: e.nop())

    with es:
        sems = {s: es.enter_context(nc.semaphore(s)) for s in
                ("sG", "sYL", "sYS", "sT", "sV", "sS", "sC")}
        with nc.Block() as block:
            kb.replay(block, sems)
    return nc


def _repack_proj(w):
    cin, cout = w.shape
    r = np.asarray(w, np.float32).reshape(cin // 128, 128, cout // 128, 128)
    return np.ascontiguousarray(r.transpose(2, 1, 0, 3))


def _repack_gu(w):
    h, d = w.shape
    r = np.asarray(w, np.float32).reshape(h // 128, 128, d // 128, 128)
    return np.ascontiguousarray(r.transpose(0, 3, 2, 1))


def _repack_wp(w):
    dout, hin = w.shape
    r = np.asarray(w, np.float32).reshape(dout // 128, 128, hin // 128, 128)
    return np.ascontiguousarray(r.transpose(2, 3, 0, 1))


def _tables(T=4096, H=128):
    fraction = np.arange(0, H, 2, dtype=np.float64) / H
    timescale = 10000.0 ** fraction
    sinusoid = np.arange(T, dtype=np.float64)[:, None] / timescale[None, :]
    sinusoid = np.concatenate([sinusoid, sinusoid], axis=-1)
    return (np.ascontiguousarray(np.sin(sinusoid).T.astype(np.float32)),
            np.ascontiguousarray(np.cos(sinusoid).T.astype(np.float32)))


def _enable_ldw_opt():
    try:
        from concourse.compiler_utils import get_compiler_flags, set_compiler_flags
        fl = [f.replace("--enable-ldw-opt=false", "--enable-ldw-opt=true")
              for f in get_compiler_flags()]
        set_compiler_flags(fl)
    except Exception:
        pass


def kernel(L0_attn_x, L0_attn_Wq, L0_attn_Wk, L0_attn_Wv, L0_attn_Wout,
           L0_ffn_w_gate_up, L0_ffn_w_proj):
    _enable_ldw_opt()
    x = np.asarray(L0_attn_x, dtype=np.float32)
    B, Tt, C = x.shape
    xt_full = np.ascontiguousarray(x.reshape(B * Tt, C).T)

    shared = {
        "wq": _repack_proj(L0_attn_Wq), "wk": _repack_proj(L0_attn_Wk),
        "wv": _repack_proj(L0_attn_Wv), "wo": _repack_proj(L0_attn_Wout),
        "ones1": np.ones((128, 1), np.float32),
        "onesk1": np.ones((1, 128), np.float32),
        "eye": np.eye(128, dtype=np.float32),
    }
    wgu_r = _repack_gu(L0_ffn_w_gate_up).reshape(128, 128, 2048)
    wp_r = _repack_wp(L0_ffn_w_proj).reshape(64, 128, 2048)
    wall = np.concatenate([wgu_r, wp_r], axis=0)
    wshards = [np.ascontiguousarray(wall[24 * c:24 * (c + 1)]) for c in range(8)]
    sel2 = np.zeros((2, 128), np.float32)
    sel2[0, :64] = 1.0
    sel2[1, 64:] = 1.0
    segc = np.zeros((128, 2), np.float32)
    segc[:64, 0] = 1.0
    segc[64:, 1] = 1.0
    rmat = np.zeros((128, 128), np.float32)
    for m in range(64):
        rmat[m + 64, m] = -1.0
        rmat[m, m + 64] = 1.0
    cvec = np.zeros((128, 16), np.float32)
    for i, c in enumerate(CONSTS):
        cvec[:, i] = c
    shared.update({"sel2": sel2, "segc": segc, "rmat": rmat, "cvec": cvec})
    sint_t, cost_t = _tables()

    nc = build(_DBG)
    in_maps = []
    for core in range(8):
        b, h = core // 2, core % 2
        base = b * Tt
        own = xt_full[:, base + h * 2048: base + (h + 1) * 2048]
        par = xt_full[:, base + (1 - h) * 2048: base + (2 - h) * 2048]
        pos = np.concatenate([np.arange(h * 2048, (h + 1) * 2048),
                              np.arange((1 - h) * 2048, (2 - h) * 2048)])
        im = dict(shared)
        im["wshard"] = wshards[core]
        im["xt"] = np.ascontiguousarray(np.concatenate([own, par], axis=1))
        im["cost"] = np.ascontiguousarray(cost_t[:, pos])
        im["sint"] = np.ascontiguousarray(sint_t[:, pos])
        in_maps.append(im)
    res = run_bass_kernel_spmd(nc, in_maps, core_ids=list(range(8)))
    out = np.empty((B, Tt, C), np.float32)
    for core in range(8):
        b, h = core // 2, core % 2
        out[b, h * 2048:(h + 1) * 2048, :] = res.results[core]["o"].T
    return out


# version 09
# speedup vs baseline: 1.0318x; 1.0151x over previous
import numpy as np
import ml_dtypes
from contextlib import ExitStack

import concourse.bass as bass
import concourse.mybir as mybir
from concourse.bass_utils import run_bass_kernel_spmd

bf16 = ml_dtypes.bfloat16
F32 = mybir.dt.float32
BF16 = mybir.dt.bfloat16
U8 = mybir.dt.uint8
AF = mybir.ActivationFunctionType
ALU = mybir.AluOpType

ENGINES = ("tensor", "vector", "scalar", "gpsimd", "sync")

C_LN_EPS = float(np.float32(1e-5))
C_EXP_BIAS = float(np.float32(-24.0))
C_Q_SC = float(np.float32(128.0 / 2048.0))
C_Q_BI = float(np.float32(128.0 * 1e-6))
C_K_SC = float(np.float32(1.0 / 512.0))
C_K_BI = float(np.float32(1e-6))
C_H_SC = float(np.float32(1.0 / 8192.0))
C_H_BI = float(np.float32(np.finfo(np.float32).eps))
C_Q64 = float(np.float32(1.0 / 64.0))
C_CLIP8 = float(np.asarray(1e-8, dtype=np.float32).astype(bf16).astype(np.float32))
CONSTS = [C_LN_EPS, C_EXP_BIAS, C_Q_SC, C_Q_BI, C_K_SC, C_K_BI, C_H_SC, C_H_BI, C_Q64]

_DBG = ()


class KB:

    def __init__(self):
        self.nc = bass.Bass("TRN2", target_bir_lowering=False)
        self.ops = {e: [] for e in ENGINES}
        self.n = {s: 0 for s in ("sG", "sYL", "sYS", "sT", "sV", "sS", "sC")}
        self.trk = {}
        self.pend = {e: [] for e in ENGINES}

    def wait(self, eng, sem, val):
        if val and val > 0:
            self.pend[eng].append((sem, val))

    def emit(self, eng, fn, inc=None, amt=1):
        waits, self.pend[eng] = self.pend[eng], []
        post = None
        if inc is not None:
            self.n[inc] += amt
            post = self.n[inc]
        self.ops[eng].append((waits, fn, inc, amt))
        return post

    def dma(self, queue, dst, src, sem):
        return self.emit(queue, lambda e, d=dst, s=src: e.dma_start(d, s), inc=sem, amt=16)

    def mm(self, out, lhsT, rhs, start, stop):
        import traceback
        site = traceback.extract_stack()[-2]
        def f(e, o=out, l=lhsT, r=rhs, a=start, b=stop, si=site):
            try:
                return e.matmul(o, l, r, start=a, stop=b)
            except Exception:
                print("MM FAIL at", si.filename, si.lineno, "lhsT", l.shape, "rhs", r.shape,
                      "out", o.shape)
                raise
        return self.emit("tensor", f, inc="sT")

    def barrier(self):
        snap = [(s, v) for s, v in self.n.items() if v > 0]
        for e in ENGINES:
            self.pend[e].extend(snap)

    def replay(self, block, sems):
        def mk(stream):
            def body(e):
                for waits, fn, inc, amt in stream:
                    for s, v in waits:
                        e.wait_ge(sems[s], v)
                    inst = fn(e)
                    if inc is not None:
                        if inst is None:
                            inst = e.nop()
                        inst.then_inc(sems[inc], amt)
            return body
        for name in ENGINES:
            if self.ops[name]:
                getattr(block, name)(mk(self.ops[name]))


def build(dbg=()):
    kb = KB()
    nc = kb.nc
    T = kb.trk

    xt = nc.dram_tensor("xt", [2048, 4096], F32, kind="ExternalInput")
    wq = nc.dram_tensor("wq", [16, 128, 16, 128], F32, kind="ExternalInput")
    wk = nc.dram_tensor("wk", [4, 128, 16, 128], F32, kind="ExternalInput")
    wv = nc.dram_tensor("wv", [4, 128, 16, 128], F32, kind="ExternalInput")
    wo = nc.dram_tensor("wo", [16, 128, 16, 128], F32, kind="ExternalInput")
    wshard = nc.dram_tensor("wshard", [24, 128, 2048], F32, kind="ExternalInput")
    cost = nc.dram_tensor("cost", [128, 4096], F32, kind="ExternalInput")
    sint = nc.dram_tensor("sint", [128, 4096], F32, kind="ExternalInput")
    ones1 = nc.dram_tensor("ones1", [128, 1], F32, kind="ExternalInput")
    onesk1 = nc.dram_tensor("onesk1", [1, 128], F32, kind="ExternalInput")
    sel2 = nc.dram_tensor("sel2", [2, 128], F32, kind="ExternalInput")
    segc = nc.dram_tensor("segc", [128, 2], F32, kind="ExternalInput")
    eye = nc.dram_tensor("eye", [128, 128], F32, kind="ExternalInput")
    rmat = nc.dram_tensor("rmat", [128, 128], F32, kind="ExternalInput")
    cvec = nc.dram_tensor("cvec", [128, 16], F32, kind="ExternalInput")
    o = nc.dram_tensor("o", [2048, 2048], F32, kind="ExternalOutput")

    def scratch(name, shape, dt_, flag):
        return nc.dram_tensor(name, shape, dt_,
                              kind="ExternalOutput" if flag in dbg else "Internal")
    s_xn = scratch("s_xn", [16, 128, 2048], BF16, "xn")
    s_enc = scratch("s_enc", [16, 128, 2048], BF16, "enc")
    s_attn = scratch("s_attn", [16, 128, 2048], BF16, "attn")
    q_in = nc.dram_tensor("q_in", [24, 128, 2048], BF16)
    q_out = nc.dram_tensor("q_out", [192, 128, 2048], BF16, addr_space="Shared")
    bar_in = nc.dram_tensor("bar_in", [1, 128], F32)
    bar_out = nc.dram_tensor("bar_out", [1, 128], F32, addr_space="Shared")
    d_qo = (nc.dram_tensor("d_qo", [192, 128, 2048], BF16, kind="ExternalOutput")
            if "wgu" in dbg else None)
    d_kqv = None
    if "kqv" in dbg:
        d_kqv = (nc.dram_tensor("d_kp", [4, 128, 4096], BF16, kind="ExternalOutput"),
                 nc.dram_tensor("d_qp", [16, 128, 2048], BF16, kind="ExternalOutput"),
                 nc.dram_tensor("d_vt", [32, 128, 512], BF16, kind="ExternalOutput"))
    d_h = (nc.dram_tensor("d_h", [64, 128, 1024], BF16, kind="ExternalOutput")
           if "h" in dbg else None)

    es = ExitStack()
    sbt = lambda n_, s_, d_: es.enter_context(nc.sbuf_tensor(n_, s_, d_))

    AR1 = sbt("AR1", [128, 65536], U8)
    AR2 = sbt("AR2", [128, 65536], U8)
    AR3 = sbt("AR3", [128, 32768], U8)
    AR4 = sbt("AR4", [128, 32768], U8)
    AR5 = sbt("AR5", [128, 14336], U8)

    def view(ar, off, shape, dt_):
        nbytes = int(np.prod(shape[1:])) * mybir.dt.size(dt_)
        assert off % 32 == 0 and off + nbytes <= int(ar.shape[1]), (off, nbytes, ar.name)
        flat = ar.reshape([128, int(ar.shape[1])]).bitcast(dt_)
        elem = mybir.dt.size(dt_)
        ap = flat[0:shape[0], off // elem: off // elem + nbytes // elem]
        if len(shape) == 3:
            ap = ap.rearrange("p (a b) -> p a b", a=shape[1], b=shape[2])
        return ap

    ones1_b = sbt("ones1_b", [128, 1], BF16)
    onesk1_f = sbt("onesk1_f", [1, 128], F32)
    sel2_b = sbt("sel2_b", [2, 128], BF16)
    segc_b = sbt("segc_b", [128, 2], BF16)
    eye_b = sbt("eye_b", [128, 128], BF16)
    rmat_b = sbt("rmat_b", [128, 128], BF16)
    cv = sbt("cv", [128, 16], F32)
    for i, c in enumerate(CONSTS):
        nc.const_aps.register(F32, c, cv[:, i:i + 1])

    psA = es.enter_context(nc.psum_tensor("psA", [128, 2048], F32))
    psB = es.enter_context(nc.psum_tensor("psB", [128, 1536], F32))
    psT = es.enter_context(nc.psum_tensor("psT", [128, 1024], BF16))

    for dst, src in ((ones1_b, ones1), (onesk1_f, onesk1), (sel2_b, sel2),
                     (segc_b, segc), (eye_b, eye), (rmat_b, rmat),
                     (cv, cvec)):
        kb.dma("gpsimd", dst[:, :], src.ap(), "sG")
    kb.barrier()

    xt_r = xt.ap().rearrange("(kc p) t -> p kc t", p=128)

    kp = view(AR3, 0, [128, 4, 4096], BF16)
    vt = view(AR4, 0, [128, 32, 512], BF16)
    qp = view(AR2, 0, [128, 16, 2048], BF16)
    xnh = view(AR1, 0, [128, 16, 2048], BF16)
    wbuf = [view(AR5, 0, [128, 2048], BF16), view(AR5, 4096, [128, 2048], BF16)]
    sqt = [view(AR5, 8192, [128, 512], BF16), view(AR5, 9216, [128, 512], BF16)]
    m_row = view(AR5, 10240, [1, 512], F32)
    a_row = view(AR5, 12288, [1, 512], F32)
    cos_sl = view(AR5, 0, [128, 512], F32)
    sin_sl = view(AR5, 2048, [128, 512], F32)
    se_t = view(AR5, 4096, [128, 512], BF16)
    t1_t = view(AR5, 5120, [128, 512], BF16)
    t2_t = view(AR5, 6144, [128, 512], BF16)
    rope_sq = view(AR5, 7168, [128, 512], BF16)

    qbuf = [view(AR2, 51200, [128, 2048], BF16), view(AR2, 55296, [128, 2048], BF16)]
    r_tQ = view(AR2, 59392, [128, 512], BF16)
    sc_bfQ = [view(AR2, 60416, [2, 512], BF16), view(AR2, 61440, [2, 512], BF16)]
    scB = view(AR2, 62464, [128, 512], BF16)
    rcp128 = view(AR2, 63488, [128, 512], F32)

    QPAR = [0]

    def quant_tile(w_ap):
        qq = QPAR[0]
        QPAR[0] ^= 1
        sc_t = sc_bfQ[qq]
        pst = psA[:, 1024 + qq * 512: 1536 + qq * 512]
        kb.wait("scalar", "sV", T.get("q_done", 0))
        av = kb.emit("scalar", lambda e, w=w_ap: e.activation(
            r_tQ[:, :], w, AF.Abs), inc="sS")
        kb.wait("tensor", "sS", av)
        kb.wait("tensor", "sS", T.get("q_segfree", 0))
        sv = kb.mm(psB[0:2, 1024:1536], segc_b[:, :], r_tQ[:, :], True, True)
        kb.wait("scalar", "sT", sv)
        kb.wait("scalar", "sT", T.get("q_bc%d" % qq, 0))
        scv = kb.emit("scalar", lambda e, d=sc_t: e.activation(
            d[:, :], psB[0:2, 1024:1536], AF.Copy, scale=C_Q64), inc="sS")
        T["q_segfree"] = scv
        kb.wait("tensor", "sS", scv)
        kb.wait("tensor", "sV", T.get("q_pst%d" % qq, 0))
        bsv = kb.mm(pst, sel2_b[:, :], sc_t[:, :], True, True)
        T["q_bc%d" % qq] = bsv
        kb.wait("vector", "sT", bsv)
        cmv = kb.emit("vector", lambda e, p=pst: e.tensor_scalar(
            scB[:, :], p, C_CLIP8, None, ALU.max), inc="sV")
        T["q_pst%d" % qq] = cmv
        kb.emit("vector", lambda e: e.reciprocal(rcp128[:, :], scB[:, :]), inc="sV")
        kb.emit("vector", lambda e, w=w_ap: e.tensor_tensor(
            r_tQ[:, :], w, rcp128[:, :], ALU.mult), inc="sV")
        kb.emit("vector", lambda e: e.tensor_scalar(
            r_tQ[:, :], r_tQ[:, :], 192.0, None, ALU.add), inc="sV")
        kb.emit("vector", lambda e: e.tensor_scalar(
            r_tQ[:, :], r_tQ[:, :], -192.0, None, ALU.add), inc="sV")
        kb.emit("vector", lambda e: e.tensor_scalar(
            r_tQ[:, :], r_tQ[:, :], -1.0, 1.0, ALU.max, ALU.min), inc="sV")
        wv_ = kb.emit("vector", lambda e, w=w_ap: e.tensor_tensor(
            w, r_tQ[:, :], scB[:, :], ALU.mult), inc="sV")
        T["q_done"] = wv_
        return wv_

    def p1q_block(bi):
        buf = qbuf[bi % 2]
        kb.wait("gpsimd", "sYS", T.get("p1q_st%d" % (bi % 2), 0))
        ld = kb.dma("gpsimd", buf[:, :], wshard.ap()[bi], "sG")
        kb.wait("scalar", "sG", ld)
        qv = None
        for qq in range(4):
            qv = quant_tile(buf[:, qq * 512:(qq + 1) * 512])
        kb.wait("sync", "sV", qv)
        st = kb.dma("sync", q_in.ap()[bi], buf[:, :], "sYS")
        T["p1q_st%d" % (bi % 2)] = st
        T["p1q_store_last"] = st
        T["p1q_V"] = kb.n["sV"]
        T["p1q_T"] = kb.n["sT"]

    P1Q_SCHED = {ji: [ji] for ji in range(18)}
    P1Q_SCHED[18] = [18, 19]
    P1Q_SCHED[19] = [20, 21]
    P1Q_SCHED[20] = [22, 23]

    def ln_half(sh):
        for tb in range(4):
            t0 = sh * 2048 + tb * 512
            tsl = slice(tb * 512, (tb + 1) * 512)
            ldv = kb.dma("gpsimd", xnh[:, :, tsl], xt_r[:, :, t0:t0 + 512], "sG")
            kb.wait("tensor", "sG", ldv)
            kb.wait("tensor", "sV", T.get("ln_psum_free", 0))
            kb.wait("tensor", "sV", T.get("ln_apply", 0))
            sxv = None
            for kc in range(16):
                sxv = kb.mm(psB[0:1, 0:512], ones1_b[:, :], xnh[:, kc, tsl],
                            kc == 0, kc == 15)
            sxxv = None
            for kc in range(16):
                kb.wait("vector", "sG", ldv)
                kb.wait("vector", "sT", T.get("lnsq%d" % (kc % 2), 0))
                sqv = kb.emit("vector",
                              lambda e, a=xnh[:, kc, tsl], q=sqt[kc % 2][:, :]:
                              e.tensor_tensor(q, a, a, ALU.mult), inc="sV")
                kb.wait("tensor", "sV", sqv)
                sxxv = kb.mm(psB[0:1, 512:1024], ones1_b[:, :], sqt[kc % 2][:, :],
                             kc == 0, kc == 15)
                T["lnsq%d" % (kc % 2)] = sxxv
            mtv = kb.emit("vector", lambda e: e.tensor_scalar(
                m_row[:, :], psB[0:1, 0:512], 1.0 / 2048.0, None, ALU.mult), inc="sV")
            kb.wait("vector", "sT", sxxv)
            kb.emit("vector", lambda e: e.tensor_scalar(
                a_row[:, :], psB[0:1, 512:1024], 1.0 / 2048.0, None, ALU.mult), inc="sV")
            kb.emit("vector", lambda e: e.tensor_tensor(
                psB[0:1, 512:1024], m_row[:, :], m_row[:, :], ALU.mult), inc="sV")
            vv = kb.emit("vector", lambda e: e.tensor_tensor(
                a_row[:, :], a_row[:, :], psB[0:1, 512:1024], ALU.subtract), inc="sV")
            T["ln_psum_free"] = vv
            kb.wait("scalar", "sV", vv)
            sv = kb.emit("scalar", lambda e: e.activation(
                a_row[:, :], a_row[:, :], AF.Sqrt, bias=C_LN_EPS), inc="sS")
            kb.wait("tensor", "sV", mtv)
            kb.mm(psA[:, 1024:1536], onesk1_f[:, :], m_row[:, :], True, True)
            kb.wait("tensor", "sS", sv)
            biv = kb.mm(psA[:, 1536:2048], onesk1_f[:, :], a_row[:, :], True, True)
            kb.wait("vector", "sT", biv)
            rv = kb.emit("vector", lambda e: e.reciprocal(
                psA[:, 1536:2048], psA[:, 1536:2048]), inc="sV")
            lastv = None
            for kc in range(16):
                sl = xnh[:, kc, tsl]
                kb.emit("vector", lambda e, a=sl: e.tensor_tensor(
                    a, a, psA[:, 1024:1536], ALU.subtract), inc="sV")
                lastv = kb.emit("vector", lambda e, a=sl: e.tensor_tensor(
                    a, a, psA[:, 1536:2048], ALU.mult), inc="sV")
            T["ln_apply"] = lastv
            if sh == 0:
                kb.wait("sync", "sV", lastv)
                kb.dma("sync", s_xn.ap().transpose([1, 0, 2])[:, :, tsl],
                       xnh[:, :, tsl], "sYS")

    def proj_half(sh):
        jobs = [("k", m) for m in range(4)] + [("v", m) for m in range(4)]
        if sh == 0:
            jobs += [("q", m) for m in range(16)]
        for ji, (kind, m) in enumerate(jobs):
            if sh == 0:
                for bi in P1Q_SCHED.get(ji, []):
                    p1q_block(bi)
            buf = wbuf[ji % 2]
            src = {"k": wk, "v": wv, "q": wq}[kind]
            kb.wait("gpsimd", "sT", T.get("wbuf%d" % (ji % 2), 0))
            ldv = kb.dma("gpsimd", buf[:, :],
                         src.ap()[m].rearrange("p a b -> p (a b)"), "sG")
            for tb in range(4):
                tsl = slice(tb * 512, (tb + 1) * 512)
                pbank = psA[:, (tb % 2) * 512:(tb % 2) * 512 + 512]
                kb.wait("tensor", "sG", ldv)
                pv = T.get("pevac%d" % (tb % 2))
                if pv is not None:
                    kb.wait("tensor", pv[1], pv[0])
                mmv = None
                for kc in range(16):
                    mmv = kb.mm(pbank, buf[:, kc * 128:(kc + 1) * 128],
                                xnh[:, kc, tsl], kc == 0, kc == 15)
                if kind == "k":
                    dst = kp[:, m, sh * 2048 + tb * 512: sh * 2048 + (tb + 1) * 512]
                    eng = "scalar" if tb % 2 == 0 else "vector"
                    sem = "sS" if tb % 2 == 0 else "sV"
                    kb.wait(eng, "sT", mmv)
                    if eng == "scalar":
                        ev = (kb.emit(eng, lambda e, d=dst, p=pbank:
                                      e.activation(d, p, AF.Copy), inc=sem), sem)
                    else:
                        ev = (kb.emit(eng, lambda e, d=dst, p=pbank:
                                      e.tensor_copy(d, p), inc=sem), sem)
                elif kind == "q":
                    dst = qp[:, m, tsl]
                    eng = "scalar" if tb % 2 == 0 else "vector"
                    sem = "sS" if tb % 2 == 0 else "sV"
                    kb.wait(eng, "sT", mmv)
                    if m >= 12:
                        kb.wait(eng, "sYS", T.get("p1q_store_last", 0))
                        kb.wait(eng, "sV", T.get("p1q_V", 0))
                        kb.wait(eng, "sT", T.get("p1q_T", 0))
                    if eng == "scalar":
                        ev = (kb.emit(eng, lambda e, d=dst, p=pbank:
                                      e.activation(d, p, AF.Copy), inc=sem), sem)
                    else:
                        ev = (kb.emit(eng, lambda e, d=dst, p=pbank:
                                      e.tensor_copy(d, p), inc=sem), sem)
                else:
                    vtile = sqt[tb % 2]
                    kb.wait("scalar", "sT", mmv)
                    kb.wait("scalar", "sV", T.get("vtile%d" % (tb % 2), 0))
                    ev0 = kb.emit("scalar", lambda e, d=vtile[:, :], p=pbank:
                                  e.activation(d, p, AF.Copy), inc="sS")
                    ev = (ev0, "sS")
                    kb.wait("tensor", "sS", ev0)
                    kb.wait("tensor", "sV", T.get("psT_free", 0))
                    tv = None
                    for jj in range(4):
                        tv = kb.emit("tensor", lambda e,
                                     oo=psT[:, jj * 128:(jj + 1) * 128],
                                     ii=vtile[:, jj * 128:(jj + 1) * 128]:
                                     e.transpose(oo, ii, eye_b[:, :]), inc="sT")
                    dst = vt[:, sh * 16 + tb * 4: sh * 16 + (tb + 1) * 4,
                             m * 128:(m + 1) * 128]
                    kb.wait("vector", "sT", tv)
                    cpv = kb.emit("vector", lambda e, d=dst,
                                  s=psT[:, 0:512].rearrange("p (a b) -> p a b", a=4, b=128):
                                  e.tensor_copy(d, s), inc="sV")
                    T["psT_free"] = cpv
                    T["vtile%d" % (tb % 2)] = cpv
                T["pevac%d" % (tb % 2)] = ev
            T["wbuf%d" % (ji % 2)] = kb.n["sT"]

    def rope(kind):
        nch = 4 if kind == "k" else 16
        buf3 = kp if kind == "k" else qp
        sc, bi = (C_K_SC, C_K_BI) if kind == "k" else (C_Q_SC, C_Q_BI)
        for tb in range(8 if kind == "k" else 4):
            tsl = slice(tb * 512, (tb + 1) * 512)
            kb.wait("sync", "sV", T.get("rope_cs", 0))
            kb.dma("sync", cos_sl[:, :], cost.ap()[:, tsl], "sYL")
            ls = kb.dma("sync", sin_sl[:, :], sint.ap()[:, tsl], "sYL")
            smv = None
            for ch in range(nch):
                kb.wait("vector", "sT", T.get("rope_sq", 0))
                qv = kb.emit("vector", lambda e, a=buf3[:, ch, tsl], q=rope_sq[:, :]:
                             e.tensor_tensor(q, a, a, ALU.mult), inc="sV")
                kb.wait("tensor", "sV", qv)
                smv = kb.mm(psB[0:1, 0:512], ones1_b[:, :], rope_sq[:, :],
                            ch == 0, ch == nch - 1)
                T["rope_sq"] = smv
            kb.wait("scalar", "sT", smv)
            sv = kb.emit("scalar", lambda e, s=sc, b=bi: e.activation(
                m_row[:, :], psB[0:1, 0:512], AF.Sqrt, bias=b, scale=s), inc="sS")
            kb.wait("vector", "sS", sv)
            rv = kb.emit("vector", lambda e: e.reciprocal(m_row[:, :], m_row[:, :]),
                         inc="sV")
            kb.wait("tensor", "sV", rv)
            kb.wait("tensor", "sS", sv)
            bv = kb.mm(psA[:, 1024:1536], onesk1_f[:, :], m_row[:, :], True, True)
            kb.wait("vector", "sT", bv)
            kb.wait("vector", "sYL", ls)
            cev = kb.emit("vector", lambda e: e.tensor_tensor(
                psB[:, 512:1024], cos_sl[:, :], psA[:, 1024:1536], ALU.mult), inc="sV")
            sev = kb.emit("vector", lambda e: e.tensor_tensor(
                se_t[:, :], sin_sl[:, :], psA[:, 1024:1536], ALU.mult), inc="sV")
            T["rope_cs"] = sev
            for ch in range(nch):
                sl = buf3[:, ch, tsl]
                kb.wait("tensor", "sV", T.get("rope_w", 0))
                rotv = kb.mm(psA[:, (ch % 2) * 512:(ch % 2) * 512 + 512],
                             rmat_b[:, :], sl, True, True)
                kb.wait("vector", "sV", cev)
                kb.emit("vector", lambda e, a=sl: e.tensor_tensor(
                    t1_t[:, :], a, psB[:, 512:1024], ALU.mult), inc="sV")
                kb.wait("vector", "sT", rotv)
                kb.emit("vector", lambda e, c=ch: e.tensor_tensor(
                    t2_t[:, :], psA[:, (c % 2) * 512:(c % 2) * 512 + 512],
                    se_t[:, :], ALU.mult), inc="sV")
                wv_ = kb.emit("vector", lambda e, a=sl: e.tensor_tensor(
                    a, t1_t[:, :], t2_t[:, :], ALU.add), inc="sV")
                T["rope_w"] = wv_

    for sh in (0, 1):
        ln_half(sh)
        kb.barrier()
        proj_half(sh)
        kb.barrier()
    kb.wait("gpsimd", "sYS", T["p1q_store_last"])
    kb.emit("gpsimd", lambda e: e.collective_compute(
        "AllGather", ALU.bypass, replica_groups=[list(range(8))],
        ins=[q_in.ap().opt()], outs=[q_out.ap().opt()]), inc="sC")
    kb.emit("gpsimd", lambda e: e.collective_compute(
        "AllReduce", ALU.add, replica_groups=[list(range(8))],
        ins=[bar_in.ap().opt()], outs=[bar_out.ap().opt()]), inc="sC")
    if d_qo is not None:
        kb.wait("gpsimd", "sC", 2)
        for i_ in range(6):
            kb.dma("gpsimd", d_qo.ap()[i_ * 32:(i_ + 1) * 32],
                   q_out.ap()[i_ * 32:(i_ + 1) * 32], "sG")
    rope("k")
    rope("q")
    kb.barrier()

    if d_kqv is not None:
        d_kp, d_qp, d_vt = d_kqv
        for m in range(4):
            kb.dma("sync", d_kp.ap()[m], kp[:, m, :], "sYS")
        for m in range(16):
            kb.dma("sync", d_qp.ap()[m], qp[:, m, :], "sYS")
        for j in range(32):
            kb.dma("sync", d_vt.ap()[j], vt[:, j, :], "sYS")
        kb.barrier()

    E_s = view(AR1, 0, [128, 32, 512], BF16)
    enc_qb = view(AR1, 32768, [128, 16, 512], BF16)
    rb_s = view(AR5, 0, [128, 512], F32)
    rs_row = view(AR5, 2048, [1, 512], F32)

    def pv_pair(g, n, g_kv):
        kb.wait("tensor", "sS", T["exp%d" % g])
        if g == 0:
            kb.wait("tensor", "sV", T.get("enc_done", 0))
        for jj in range(2):
            j = 2 * g + jj
            kb.mm(psB[:, 0:512], vt[:, j, g_kv * 128:(g_kv + 1) * 128],
                  E_s[:, j, :], j == 0, j == 31)

    pending_norm = []

    def flush_norm():
        while pending_norm:
            fn = pending_norm.pop(0)
            fn()

    for qb in range(4):
        qs = slice(qb * 512, (qb + 1) * 512)
        for n in range(16):
            g_kv = n // 4
            prev_rsA = T.get("rs_endA", 0)
            prev_rsB = T.get("rs_endB", 0)
            for g in range(16):
                grp = psA[:, (g % 2) * 1024:(g % 2) * 1024 + 1024]
                kb.wait("tensor", "sS", T.get("qkgrp%d" % (g % 2), 0))
                for jj in range(2):
                    j = 2 * g + jj
                    kb.mm(grp[:, jj * 512:(jj + 1) * 512],
                          kp[:, g_kv, j * 128:(j + 1) * 128], qp[:, n, qs],
                          True, True)
                if g == 0:
                    flush_norm()
                kb.wait("scalar", "sT", kb.n["sT"])
                kb.wait("scalar", "sT", prev_rsA if g < 8 else prev_rsB)
                ev = kb.emit("scalar", lambda e, g_=g, gr=grp: e.activation(
                    E_s[:, 2 * g_:2 * g_ + 2, :].rearrange("p a b -> p (a b)"),
                    gr, AF.Exp, bias=C_EXP_BIAS), inc="sS")
                T["exp%d" % g] = ev
                T["qkgrp%d" % (g % 2)] = ev
                if g >= 1:
                    pv_pair(g - 1, n, g_kv)
            pv_pair(15, n, g_kv)
            kb.wait("tensor", "sV", T.get("rs_done", 0))
            for j in range(16):
                kb.mm(psB[0:1, 512:1024], ones1_b[:, :], E_s[:, j, :],
                      j == 0, False)
            T["rs_endA"] = kb.n["sT"]
            for j in range(16, 32):
                kb.mm(psB[0:1, 512:1024], ones1_b[:, :], E_s[:, j, :],
                      False, j == 31)
            T["rs_endB"] = kb.n["sT"]
            kb.wait("vector", "sT", kb.n["sT"])
            rv = kb.emit("vector", lambda e: e.tensor_copy(
                rs_row[:, :], psB[0:1, 512:1024]), inc="sV")
            T["rs_done"] = rv

            def mk_norm(n_=n, rv_=rv):
                def do():
                    kb.wait("tensor", "sV", rv_)
                    bv = kb.mm(psB[:, 1024:1536], onesk1_f[:, :], rs_row[:, :],
                               True, True)
                    kb.wait("vector", "sT", bv)
                    kb.emit("vector", lambda e: e.reciprocal(
                        psB[:, 1024:1536], psB[:, 1024:1536]), inc="sV")
                    kb.emit("vector", lambda e: e.tensor_copy(
                        rb_s[:, :], psB[:, 1024:1536]), inc="sV")
                    env = kb.emit("vector", lambda e: e.tensor_tensor(
                        enc_qb[:, n_, :], psB[:, 0:512], rb_s[:, :], ALU.mult),
                        inc="sV")
                    T["enc_done"] = env
                return do
            pending_norm.append(mk_norm())
        flush_norm()
        kb.wait("sync", "sV", kb.n["sV"])
        kb.dma("sync", s_enc.ap().transpose([1, 0, 2])[:, :, qs],
               enc_qb[:, :, :], "sYS")
    kb.barrier()

    enc_full = view(AR1, 0, [128, 16, 2048], BF16)
    wobuf = [view(AR3, 0, [128, 2048], BF16), view(AR3, 4096, [128, 2048], BF16)]
    attn_t = [view(AR5, 8192, [128, 512], BF16), view(AR5, 9216, [128, 512], BF16)]
    lenc = kb.dma("sync", enc_full[:, :, :], s_enc.ap().transpose([1, 0, 2]), "sYL")
    for m in range(16):
        kb.wait("gpsimd", "sT", T.get("wobuf%d" % (m % 2), 0))
        ldv = kb.dma("gpsimd", wobuf[m % 2][:, :],
                     wo.ap()[m].rearrange("p a b -> p (a b)"), "sG")
        for qb in range(4):
            pbank = psA[:, (qb % 2) * 512:(qb % 2) * 512 + 512]
            kb.wait("tensor", "sG", ldv)
            kb.wait("tensor", "sYL", lenc)
            kb.wait("tensor", "sS", T.get("aevac%d" % (qb % 2), 0))
            mmv = None
            for kc in range(16):
                mmv = kb.mm(pbank, wobuf[m % 2][:, kc * 128:(kc + 1) * 128],
                            enc_full[:, kc, qb * 512:(qb + 1) * 512], kc == 0, kc == 15)
            kb.wait("scalar", "sT", mmv)
            kb.wait("scalar", "sYS", T.get("ast%d" % (qb % 2), 0))
            ev = kb.emit("scalar", lambda e, d=attn_t[qb % 2][:, :], p=pbank:
                         e.activation(d, p, AF.Copy), inc="sS")
            T["aevac%d" % (qb % 2)] = ev
            kb.wait("sync", "sS", ev)
            stv = kb.dma("sync", s_attn.ap()[m][:, qb * 512:(qb + 1) * 512],
                         attn_t[qb % 2][:, :], "sYS")
            T["ast%d" % (qb % 2)] = stv
        T["wobuf%d" % (m % 2)] = kb.n["sT"]
    kb.barrier()

    h_lo = view(AR1, 0, [128, 32, 1024], BF16)
    h_hi = view(AR2, 0, [128, 32, 1024], BF16)
    xn_th = view(AR4, 0, [128, 16, 1024], BF16)
    gbuf = [view(AR3, 0, [128, 2048], BF16), view(AR3, 4096, [128, 2048], BF16)]
    ubuf = [view(AR3, 8192, [128, 2048], BF16), view(AR3, 12288, [128, 2048], BF16)]
    sg_t = view(AR3, 16384, [128, 512], BF16)
    hq_t = view(AR3, 17408, [128, 512], BF16)
    ib_s = view(AR5, 0, [128, 1024], F32)
    of_s = view(AR5, 4096, [128, 1024], F32)
    at_t = view(AR5, 8192, [128, 1024], BF16)
    x_t = view(AR5, 10240, [128, 1024], F32)
    ih_row = view(AR5, 10240, [1, 1024], F32)

    def h_slice(mp, qoff, th_unused=None):
        base = h_lo if mp < 32 else h_hi
        return base[:, mp % 32, qoff:qoff + 512]

    for th in range(2):
        tsl = slice(th * 1024, (th + 1) * 1024)
        kb.wait("sync", "sT", T.get("xnth_free", 0))
        lxn = kb.dma("sync", xn_th[:, :, :],
                     s_xn.ap().transpose([1, 0, 2])[:, :, tsl], "sYL")
        for mp in range(64):
            bg, bu = gbuf[mp % 2], ubuf[mp % 2]
            kb.wait("sync", "sT", T.get("gumm%d" % (mp % 2), 0))
            if th == 0 and mp == 0:
                kb.wait("sync", "sC", 2)
            kb.dma("sync", bg[:, :], q_out.ap()[mp], "sYL")
            lu = kb.dma("sync", bu[:, :], q_out.ap()[64 + mp], "sYL")
            kb.wait("tensor", "sYL", lu)
            kb.wait("tensor", "sYL", lxn)
            for part, wt, bb, qoff in (("u0", bu, 2, 0), ("u1", bu, 3, 512),
                                       ("g0", bg, 0, 0), ("g1", bg, 1, 512)):
                pbank = psA[:, bb * 512:(bb + 1) * 512]
                kb.wait("tensor", "sV", T.get("hfree%d" % bb, 0))
                kb.wait("tensor", "sS", T.get("sfree%d" % bb, 0))
                for kc in range(16):
                    kb.mm(pbank, wt[:, kc * 128:(kc + 1) * 128],
                          xn_th[:, kc, qoff:qoff + 512], kc == 0, kc == 15)
                T["gum_%s" % part] = kb.n["sT"]
            for ih in range(2):
                bg_b, bu_b = ih, 2 + ih
                qoff = ih * 512
                kb.wait("scalar", "sT", T["gum_g%d" % ih])
                kb.wait("scalar", "sV", T.get("sg_free", 0))
                sgv = kb.emit("scalar", lambda e, b=bg_b: e.activation(
                    sg_t[:, :], psA[:, b * 512:(b + 1) * 512], AF.Silu), inc="sS")
                T["sfree%d" % bg_b] = sgv
                kb.wait("vector", "sS", sgv)
                kb.wait("vector", "sT", T["gum_u%d" % ih])
                hv = kb.emit("vector", lambda e, b=bu_b, d=h_slice(mp, qoff):
                             e.tensor_tensor(d, sg_t[:, :],
                                             psA[:, b * 512:(b + 1) * 512], ALU.mult),
                             inc="sV")
                T["hfree%d" % bu_b] = hv
                T["hfree%d" % bg_b] = hv
                T["sg_free"] = hv
                kb.wait("vector", "sT", T.get("hq_free", 0))
                hqv = kb.emit("vector", lambda e, d=h_slice(mp, qoff):
                              e.tensor_tensor(hq_t[:, :], d, d, ALU.mult), inc="sV")
                kb.wait("tensor", "sV", hqv)
                hsv = kb.mm(psB[0:1, qoff:qoff + 512], ones1_b[:, :], hq_t[:, :],
                            mp == 0, mp == 63)
                T["hq_free"] = hsv
            T["gumm%d" % (mp % 2)] = kb.n["sT"]
            if d_h is not None and th == 0:
                kb.wait("sync", "sV", kb.n["sV"])
                kb.dma("sync", d_h.ap()[mp], h_slice(mp, 0, None).tensor[
                    :, mp % 32, :] if False else h_lo[:, mp, :] if mp < 32
                    else h_hi[:, mp - 32, :], "sYS")
        kb.barrier()
        sv = kb.emit("scalar", lambda e: e.activation(
            ih_row[:, :], psB[0:1, 0:1024], AF.Sqrt, bias=C_H_BI, scale=C_H_SC),
            inc="sS")
        kb.wait("vector", "sS", sv)
        rv = kb.emit("vector", lambda e: e.reciprocal(ih_row[:, :], ih_row[:, :]),
                     inc="sV")
        kb.wait("tensor", "sV", rv)
        kb.mm(psB[:, 0:512], onesk1_f[:, :], ih_row[:, 0:512], True, True)
        b2 = kb.mm(psB[:, 512:1024], onesk1_f[:, :], ih_row[:, 512:1024], True, True)
        kb.wait("vector", "sT", b2)
        kb.emit("vector", lambda e: e.tensor_copy(ib_s[:, :], psB[:, 0:1024]), inc="sV")
        kb.barrier()
        wp_sw = view(AR3, 0, [128, 64, 256], BF16)
        for sw in range(16):
            par = sw % 2
            if par == 0:
                kb.wait("sync", "sT", T.get("sw_free1", 0))
                lw = kb.dma("sync", wp_sw[:, :, :],
                            q_out.ap()[128:192, :, (sw // 2) * 256:(sw // 2 + 1) * 256]
                            .transpose([1, 0, 2]), "sYL")
                T["wpsw_ld"] = lw
            kb.wait("tensor", "sYL", T["wpsw_ld"])
            kb.wait("tensor", "sV", T.get("of_done%d" % par, 0))
            for kc in range(64):
                for t2 in range(2):
                    kb.mm(psA[:, par * 1024 + t2 * 512: par * 1024 + (t2 + 1) * 512],
                          wp_sw[:, kc, par * 128:(par + 1) * 128],
                          h_slice(kc, t2 * 512), kc == 0, kc == 63)
            T["sw_free%d" % par] = kb.n["sT"]
            mg = sw
            kb.wait("sync", "sV", T.get("at_free", 0))
            la = kb.dma("sync", at_t[:, :], s_attn.ap()[mg][:, tsl], "sYL")
            lx = kb.dma("sync", x_t[:, :], xt.ap()[mg * 128:(mg + 1) * 128, tsl],
                        "sYL")
            kb.wait("vector", "sT", T["sw_free%d" % par])
            kb.wait("vector", "sYS", T.get("o_st", 0))
            kb.emit("vector", lambda e, p_=par: e.tensor_tensor(
                of_s[:, :], psA[:, p_ * 1024:(p_ + 1) * 1024], ib_s[:, :],
                ALU.mult), inc="sV")
            kb.wait("vector", "sYL", la)
            av2 = kb.emit("vector", lambda e: e.tensor_tensor(
                of_s[:, :], of_s[:, :], at_t[:, :], ALU.add), inc="sV")
            T["at_free"] = av2
            kb.wait("vector", "sYL", lx)
            ofv = kb.emit("vector", lambda e: e.tensor_tensor(
                of_s[:, :], of_s[:, :], x_t[:, :], ALU.add), inc="sV")
            T["of_done%d" % par] = ofv
            kb.wait("sync", "sV", ofv)
            ostv = kb.dma("sync", o.ap()[mg * 128:(mg + 1) * 128, tsl],
                          of_s[:, :], "sYS")
            T["o_st"] = ostv
            T["xnth_free"] = kb.n["sT"]
        kb.barrier()

    kb.wait("sync", "sYS", kb.n["sYS"])
    kb.emit("sync", lambda e: e.nop())
    kb.wait("gpsimd", "sYS", kb.n["sYS"])
    kb.emit("gpsimd", lambda e: e.nop())

    with es:
        sems = {s: es.enter_context(nc.semaphore(s)) for s in
                ("sG", "sYL", "sYS", "sT", "sV", "sS", "sC")}
        with nc.Block() as block:
            kb.replay(block, sems)
    return nc


def _repack_proj(w):
    cin, cout = w.shape
    r = np.asarray(w, np.float32).reshape(cin // 128, 128, cout // 128, 128)
    return np.ascontiguousarray(r.transpose(2, 1, 0, 3))


def _repack_gu(w):
    h, d = w.shape
    r = np.asarray(w, np.float32).reshape(h // 128, 128, d // 128, 128)
    return np.ascontiguousarray(r.transpose(0, 3, 2, 1))


def _repack_wp(w):
    dout, hin = w.shape
    r = np.asarray(w, np.float32).reshape(dout // 128, 128, hin // 128, 128)
    return np.ascontiguousarray(r.transpose(2, 3, 0, 1))


def _tables(T=4096, H=128):
    fraction = np.arange(0, H, 2, dtype=np.float64) / H
    timescale = 10000.0 ** fraction
    sinusoid = np.arange(T, dtype=np.float64)[:, None] / timescale[None, :]
    sinusoid = np.concatenate([sinusoid, sinusoid], axis=-1)
    return (np.ascontiguousarray(np.sin(sinusoid).T.astype(np.float32)),
            np.ascontiguousarray(np.cos(sinusoid).T.astype(np.float32)))


def kernel(L0_attn_x, L0_attn_Wq, L0_attn_Wk, L0_attn_Wv, L0_attn_Wout,
           L0_ffn_w_gate_up, L0_ffn_w_proj):
    x = np.asarray(L0_attn_x, dtype=np.float32)
    B, Tt, C = x.shape
    xt_full = np.ascontiguousarray(x.reshape(B * Tt, C).T)

    shared = {
        "wq": _repack_proj(L0_attn_Wq), "wk": _repack_proj(L0_attn_Wk),
        "wv": _repack_proj(L0_attn_Wv), "wo": _repack_proj(L0_attn_Wout),
        "ones1": np.ones((128, 1), np.float32),
        "onesk1": np.ones((1, 128), np.float32),
        "eye": np.eye(128, dtype=np.float32),
    }
    wgu_r = _repack_gu(L0_ffn_w_gate_up).reshape(128, 128, 2048)
    wp_r = _repack_wp(L0_ffn_w_proj).reshape(64, 128, 2048)
    wall = np.concatenate([wgu_r, wp_r], axis=0)
    wshards = [np.ascontiguousarray(wall[24 * c:24 * (c + 1)]) for c in range(8)]
    sel2 = np.zeros((2, 128), np.float32)
    sel2[0, :64] = 1.0
    sel2[1, 64:] = 1.0
    segc = np.zeros((128, 2), np.float32)
    segc[:64, 0] = 1.0
    segc[64:, 1] = 1.0
    rmat = np.zeros((128, 128), np.float32)
    for m in range(64):
        rmat[m + 64, m] = -1.0
        rmat[m, m + 64] = 1.0
    cvec = np.zeros((128, 16), np.float32)
    for i, c in enumerate(CONSTS):
        cvec[:, i] = c
    shared.update({"sel2": sel2, "segc": segc, "rmat": rmat, "cvec": cvec})
    sint_t, cost_t = _tables()

    nc = build(_DBG)
    in_maps = []
    for core in range(8):
        b, h = core // 2, core % 2
        base = b * Tt
        own = xt_full[:, base + h * 2048: base + (h + 1) * 2048]
        par = xt_full[:, base + (1 - h) * 2048: base + (2 - h) * 2048]
        pos = np.concatenate([np.arange(h * 2048, (h + 1) * 2048),
                              np.arange((1 - h) * 2048, (2 - h) * 2048)])
        im = dict(shared)
        im["wshard"] = wshards[core]
        im["xt"] = np.ascontiguousarray(np.concatenate([own, par], axis=1))
        im["cost"] = np.ascontiguousarray(cost_t[:, pos])
        im["sint"] = np.ascontiguousarray(sint_t[:, pos])
        in_maps.append(im)
    res = run_bass_kernel_spmd(nc, in_maps, core_ids=list(range(8)))
    out = np.empty((B, Tt, C), np.float32)
    for core in range(8):
        b, h = core // 2, core % 2
        out[b, h * 2048:(h + 1) * 2048, :] = res.results[core]["o"].T
    return out


# version 11
# speedup vs baseline: 1.0837x; 1.0503x over previous
import numpy as np
import ml_dtypes
from contextlib import ExitStack

import concourse.bass as bass
import concourse.mybir as mybir
from concourse.bass_utils import run_bass_kernel_spmd

bf16 = ml_dtypes.bfloat16
F32 = mybir.dt.float32
BF16 = mybir.dt.bfloat16
U8 = mybir.dt.uint8
AF = mybir.ActivationFunctionType
ALU = mybir.AluOpType

ENGINES = ("tensor", "vector", "scalar", "gpsimd", "sync")

C_LN_EPS = float(np.float32(1e-5))
C_EXP_BIAS = float(np.float32(-24.0))
C_Q_SC = float(np.float32(128.0 / 2048.0))
C_Q_BI = float(np.float32(128.0 * 1e-6))
C_K_SC = float(np.float32(1.0 / 512.0))
C_K_BI = float(np.float32(1e-6))
C_H_SC = float(np.float32(1.0 / 8192.0))
C_H_BI = float(np.float32(np.finfo(np.float32).eps))
C_Q64 = float(np.float32(1.0 / 64.0))
C_NEG1 = -1.0
C_CLIP8 = float(np.asarray(1e-8, dtype=np.float32).astype(bf16).astype(np.float32))
CONSTS = [C_LN_EPS, C_EXP_BIAS, C_Q_SC, C_Q_BI, C_K_SC, C_K_BI, C_H_SC, C_H_BI, C_Q64,
          C_NEG1]

_DBG = ()


class KB:

    def __init__(self):
        self.nc = bass.Bass("TRN2", target_bir_lowering=False)
        self.ops = {e: [] for e in ENGINES}
        self.n = {s: 0 for s in ("sG", "sYL", "sYS", "sT", "sV", "sS", "sC")}
        self.trk = {}
        self.pend = {e: [] for e in ENGINES}

    def wait(self, eng, sem, val):
        if val and val > 0:
            self.pend[eng].append((sem, val))

    def emit(self, eng, fn, inc=None, amt=1):
        waits, self.pend[eng] = self.pend[eng], []
        post = None
        if inc is not None:
            self.n[inc] += amt
            post = self.n[inc]
        self.ops[eng].append((waits, fn, inc, amt))
        return post

    def dma(self, queue, dst, src, sem):
        return self.emit(queue, lambda e, d=dst, s=src: e.dma_start(d, s), inc=sem, amt=16)

    def mm(self, out, lhsT, rhs, start, stop, inc=True):
        return self.emit("tensor",
                         lambda e, o=out, l=lhsT, r=rhs, a=start, b=stop:
                         e.matmul(o, l, r, start=a, stop=b),
                         inc="sT" if inc else None)

    def barrier(self):
        snap = [(s, v) for s, v in self.n.items() if v > 0]
        for e in ENGINES:
            self.pend[e].extend(snap)

    def replay(self, block, sems):
        def mk(stream):
            def body(e):
                for waits, fn, inc, amt in stream:
                    for s, v in waits:
                        e.wait_ge(sems[s], v)
                    inst = fn(e)
                    if inc is not None:
                        if inst is None:
                            inst = e.nop()
                        inst.then_inc(sems[inc], amt)
            return body
        for name in ENGINES:
            if self.ops[name]:
                getattr(block, name)(mk(self.ops[name]))


def build(dbg=()):
    kb = KB()
    nc = kb.nc
    T = kb.trk

    xt = nc.dram_tensor("xt", [2048, 4096], F32, kind="ExternalInput")
    wq = nc.dram_tensor("wq", [16, 128, 16, 128], F32, kind="ExternalInput")
    wk = nc.dram_tensor("wk", [4, 128, 16, 128], F32, kind="ExternalInput")
    wv = nc.dram_tensor("wv", [4, 128, 16, 128], F32, kind="ExternalInput")
    wo = nc.dram_tensor("wo", [16, 128, 16, 128], F32, kind="ExternalInput")
    wshard = nc.dram_tensor("wshard", [24, 128, 2048], F32, kind="ExternalInput")
    cost = nc.dram_tensor("cost", [128, 4096], F32, kind="ExternalInput")
    sint = nc.dram_tensor("sint", [128, 4096], F32, kind="ExternalInput")
    ones1 = nc.dram_tensor("ones1", [128, 1], F32, kind="ExternalInput")
    onesk1 = nc.dram_tensor("onesk1", [1, 128], F32, kind="ExternalInput")
    sel2 = nc.dram_tensor("sel2", [2, 128], F32, kind="ExternalInput")
    segc = nc.dram_tensor("segc", [128, 2], F32, kind="ExternalInput")
    eye = nc.dram_tensor("eye", [128, 128], F32, kind="ExternalInput")
    rmat = nc.dram_tensor("rmat", [128, 128], F32, kind="ExternalInput")
    cvec = nc.dram_tensor("cvec", [128, 16], F32, kind="ExternalInput")
    o = nc.dram_tensor("o", [2048, 2048], F32, kind="ExternalOutput")

    def scratch(name, shape, dt_, flag):
        return nc.dram_tensor(name, shape, dt_,
                              kind="ExternalOutput" if flag in dbg else "Internal")
    s_xn = scratch("s_xn", [16, 128, 2048], BF16, "xn")
    s_enc = scratch("s_enc", [16, 128, 2048], BF16, "enc")
    s_attn = scratch("s_attn", [16, 128, 2048], BF16, "attn")
    q_in = nc.dram_tensor("q_in", [24, 128, 2048], BF16)
    q_out = nc.dram_tensor("q_out", [192, 128, 2048], BF16, addr_space="Shared")
    bar_in = nc.dram_tensor("bar_in", [1, 128], F32)
    bar_out = nc.dram_tensor("bar_out", [1, 128], F32, addr_space="Shared")
    d_qo = (nc.dram_tensor("d_qo", [192, 128, 2048], BF16, kind="ExternalOutput")
            if "wgu" in dbg else None)
    d_kqv = None
    if "kqv" in dbg:
        d_kqv = (nc.dram_tensor("d_kp", [4, 128, 4096], BF16, kind="ExternalOutput"),
                 nc.dram_tensor("d_qp", [16, 128, 2048], BF16, kind="ExternalOutput"),
                 nc.dram_tensor("d_vt", [32, 128, 512], BF16, kind="ExternalOutput"))
    d_h = (nc.dram_tensor("d_h", [64, 128, 1024], BF16, kind="ExternalOutput")
           if "h" in dbg else None)

    es = ExitStack()
    sbt = lambda n_, s_, d_: es.enter_context(nc.sbuf_tensor(n_, s_, d_))

    AR1 = sbt("AR1", [128, 65536], U8)
    AR2 = sbt("AR2", [128, 65536], U8)
    AR3 = sbt("AR3", [128, 32768], U8)
    AR4 = sbt("AR4", [128, 32768], U8)
    AR5 = sbt("AR5", [128, 14336], U8)

    def view(ar, off, shape, dt_):
        nbytes = int(np.prod(shape[1:])) * mybir.dt.size(dt_)
        assert off % 32 == 0 and off + nbytes <= int(ar.shape[1]), (off, nbytes, ar.name)
        flat = ar.reshape([128, int(ar.shape[1])]).bitcast(dt_)
        elem = mybir.dt.size(dt_)
        ap = flat[0:shape[0], off // elem: off // elem + nbytes // elem]
        if len(shape) == 3:
            ap = ap.rearrange("p (a b) -> p a b", a=shape[1], b=shape[2])
        return ap

    ones1_b = sbt("ones1_b", [128, 1], BF16)
    onesk1_f = sbt("onesk1_f", [1, 128], F32)
    sel2_b = sbt("sel2_b", [2, 128], BF16)
    segc_b = sbt("segc_b", [128, 2], BF16)
    eye_b = sbt("eye_b", [128, 128], BF16)
    rmat_b = sbt("rmat_b", [128, 128], BF16)
    cv = sbt("cv", [128, 16], F32)
    for i, c in enumerate(CONSTS):
        nc.const_aps.register(F32, c, cv[:, i:i + 1])

    psA = es.enter_context(nc.psum_tensor("psA", [128, 2048], F32))
    psB = es.enter_context(nc.psum_tensor("psB", [128, 1536], F32))
    psT = es.enter_context(nc.psum_tensor("psT", [128, 1024], BF16))

    for dst, src in ((ones1_b, ones1), (onesk1_f, onesk1), (sel2_b, sel2),
                     (segc_b, segc), (eye_b, eye), (rmat_b, rmat),
                     (cv, cvec)):
        kb.dma("gpsimd", dst[:, :], src.ap(), "sG")
    kb.barrier()

    xt_r = xt.ap().rearrange("(kc p) t -> p kc t", p=128)

    kp = view(AR3, 0, [128, 4, 4096], BF16)
    vt = view(AR4, 0, [128, 32, 512], BF16)
    qp = view(AR2, 0, [128, 16, 2048], BF16)
    xnh = view(AR1, 0, [128, 16, 2048], BF16)
    wbuf = [view(AR5, 0, [128, 2048], BF16), view(AR5, 4096, [128, 2048], BF16)]
    sqt = [view(AR5, 8192, [128, 512], BF16), view(AR5, 9216, [128, 512], BF16)]
    m_row = view(AR5, 10240, [1, 512], F32)
    a_row = view(AR5, 12288, [1, 512], F32)
    cos_sl = view(AR5, 0, [128, 512], F32)
    sin_sl = view(AR5, 2048, [128, 512], F32)
    se_t = view(AR5, 4096, [128, 512], BF16)
    t1_t = view(AR5, 5120, [128, 512], BF16)
    t2_t = view(AR5, 6144, [128, 512], BF16)
    rope_sq = view(AR5, 7168, [128, 512], BF16)

    qbuf = [view(AR2, 51200, [128, 2048], BF16), view(AR2, 55296, [128, 2048], BF16)]
    r_tQ = view(AR2, 59392, [128, 512], BF16)
    sc_bfQ = [view(AR2, 60416, [2, 512], BF16), view(AR2, 61440, [2, 512], BF16)]
    scB = view(AR2, 62464, [128, 512], BF16)
    rcp128 = view(AR2, 63488, [128, 512], F32)

    QPAR = [0]

    def quant_tile(w_ap):
        qq = QPAR[0]
        QPAR[0] ^= 1
        sc_t = sc_bfQ[qq]
        pst = psA[:, 1024 + qq * 512: 1536 + qq * 512]
        kb.wait("scalar", "sV", T.get("q_done", 0))
        av = kb.emit("scalar", lambda e, w=w_ap: e.activation(
            r_tQ[:, :], w, AF.Abs), inc="sS")
        kb.wait("tensor", "sS", av)
        kb.wait("tensor", "sS", T.get("q_segfree", 0))
        sv = kb.mm(psB[0:2, 1024:1536], segc_b[:, :], r_tQ[:, :], True, True)
        kb.wait("scalar", "sT", sv)
        kb.wait("scalar", "sT", T.get("q_bc%d" % qq, 0))
        scv = kb.emit("scalar", lambda e, d=sc_t: e.activation(
            d[:, :], psB[0:2, 1024:1536], AF.Copy, scale=C_Q64), inc="sS")
        T["q_segfree"] = scv
        kb.wait("tensor", "sS", scv)
        kb.wait("tensor", "sV", T.get("q_pst%d" % qq, 0))
        bsv = kb.mm(pst, sel2_b[:, :], sc_t[:, :], True, True)
        T["q_bc%d" % qq] = bsv
        kb.wait("vector", "sT", bsv)
        cmv = kb.emit("vector", lambda e, p=pst: e.tensor_scalar(
            scB[:, :], p, C_CLIP8, None, ALU.max), inc="sV")
        T["q_pst%d" % qq] = cmv
        kb.emit("vector", lambda e: e.reciprocal(rcp128[:, :], scB[:, :]), inc="sV")
        kb.emit("vector", lambda e, w=w_ap: e.tensor_tensor(
            r_tQ[:, :], w, rcp128[:, :], ALU.mult), inc="sV")
        kb.emit("vector", lambda e: e.tensor_scalar(
            r_tQ[:, :], r_tQ[:, :], 192.0, None, ALU.add), inc="sV")
        kb.emit("vector", lambda e: e.tensor_scalar(
            r_tQ[:, :], r_tQ[:, :], -192.0, None, ALU.add), inc="sV")
        kb.emit("vector", lambda e: e.tensor_scalar(
            r_tQ[:, :], r_tQ[:, :], -1.0, 1.0, ALU.max, ALU.min), inc="sV")
        wv_ = kb.emit("vector", lambda e, w=w_ap: e.tensor_tensor(
            w, r_tQ[:, :], scB[:, :], ALU.mult), inc="sV")
        T["q_done"] = wv_
        return wv_

    def p1q_block(bi):
        buf = qbuf[bi % 2]
        kb.wait("gpsimd", "sYS", T.get("p1q_st%d" % (bi % 2), 0))
        ld = kb.dma("gpsimd", buf[:, :], wshard.ap()[bi], "sG")
        kb.wait("scalar", "sG", ld)
        qv = None
        for qq in range(4):
            qv = quant_tile(buf[:, qq * 512:(qq + 1) * 512])
        kb.wait("sync", "sV", qv)
        st = kb.dma("sync", q_in.ap()[bi], buf[:, :], "sYS")
        T["p1q_st%d" % (bi % 2)] = st
        T["p1q_store_last"] = st
        T["p1q_V"] = kb.n["sV"]
        T["p1q_T"] = kb.n["sT"]

    P1Q_SCHED = {ji: [ji] for ji in range(18)}
    P1Q_SCHED[18] = [18, 19]
    P1Q_SCHED[19] = [20, 21]
    P1Q_SCHED[20] = [22, 23]

    def ln_half(sh):
        for tb in range(4):
            t0 = sh * 2048 + tb * 512
            tsl = slice(tb * 512, (tb + 1) * 512)
            ldv = kb.dma("gpsimd", xnh[:, :, tsl], xt_r[:, :, t0:t0 + 512], "sG")
            kb.wait("tensor", "sG", ldv)
            kb.wait("tensor", "sV", T.get("ln_psum_free", 0))
            kb.wait("tensor", "sV", T.get("ln_apply", 0))
            sxv = None
            for kc in range(16):
                sxv = kb.mm(psB[0:1, 0:512], ones1_b[:, :], xnh[:, kc, tsl],
                            kc == 0, kc == 15, inc=(kc == 15))
            sxxv = None
            for kc in range(16):
                kb.wait("vector", "sG", ldv)
                kb.wait("vector", "sT", T.get("lnsq%d" % (kc % 2), 0))
                sqv = kb.emit("vector",
                              lambda e, a=xnh[:, kc, tsl], q=sqt[kc % 2][:, :]:
                              e.tensor_tensor(q, a, a, ALU.mult), inc="sV")
                kb.wait("tensor", "sV", sqv)
                sxxv = kb.mm(psB[0:1, 512:1024], ones1_b[:, :], sqt[kc % 2][:, :],
                             kc == 0, kc == 15)
                T["lnsq%d" % (kc % 2)] = sxxv
            mtv = kb.emit("vector", lambda e: e.tensor_scalar(
                m_row[:, :], psB[0:1, 0:512], 1.0 / 2048.0, None, ALU.mult), inc="sV")
            kb.wait("vector", "sT", sxxv)
            kb.emit("vector", lambda e: e.tensor_scalar(
                a_row[:, :], psB[0:1, 512:1024], 1.0 / 2048.0, None, ALU.mult), inc="sV")
            kb.emit("vector", lambda e: e.tensor_tensor(
                psB[0:1, 512:1024], m_row[:, :], m_row[:, :], ALU.mult), inc="sV")
            vv = kb.emit("vector", lambda e: e.tensor_tensor(
                a_row[:, :], a_row[:, :], psB[0:1, 512:1024], ALU.subtract), inc="sV")
            T["ln_psum_free"] = vv
            kb.wait("scalar", "sV", vv)
            sv = kb.emit("scalar", lambda e: e.activation(
                a_row[:, :], a_row[:, :], AF.Sqrt, bias=C_LN_EPS), inc="sS")
            kb.wait("tensor", "sV", mtv)
            kb.mm(psA[:, 1024:1536], onesk1_f[:, :], m_row[:, :], True, True)
            kb.wait("tensor", "sS", sv)
            biv = kb.mm(psA[:, 1536:2048], onesk1_f[:, :], a_row[:, :], True, True)
            kb.wait("vector", "sT", biv)
            rv = kb.emit("vector", lambda e: e.reciprocal(
                psA[:, 1536:2048], psA[:, 1536:2048]), inc="sV")
            lastv = None
            for kc in range(16):
                sl = xnh[:, kc, tsl]
                kb.emit("vector", lambda e, a=sl: e.tensor_tensor(
                    a, a, psA[:, 1024:1536], ALU.subtract), inc="sV")
                lastv = kb.emit("vector", lambda e, a=sl: e.tensor_tensor(
                    a, a, psA[:, 1536:2048], ALU.mult), inc="sV")
            T["ln_apply"] = lastv
            if sh == 0:
                kb.wait("sync", "sV", lastv)
                kb.dma("sync", s_xn.ap().transpose([1, 0, 2])[:, :, tsl],
                       xnh[:, :, tsl], "sYS")

    def proj_half(sh):
        jobs = [("k", m) for m in range(4)] + [("v", m) for m in range(4)]
        if sh == 0:
            jobs += [("q", m) for m in range(16)]
        for ji, (kind, m) in enumerate(jobs):
            if sh == 0:
                for bi in P1Q_SCHED.get(ji, []):
                    p1q_block(bi)
            buf = wbuf[ji % 2]
            src = {"k": wk, "v": wv, "q": wq}[kind]
            kb.wait("gpsimd", "sT", T.get("wbuf%d" % (ji % 2), 0))
            ldv = kb.dma("gpsimd", buf[:, :],
                         src.ap()[m].rearrange("p a b -> p (a b)"), "sG")
            for tb in range(4):
                tsl = slice(tb * 512, (tb + 1) * 512)
                pbank = psA[:, (tb % 2) * 512:(tb % 2) * 512 + 512]
                kb.wait("tensor", "sG", ldv)
                pv = T.get("pevac%d" % (tb % 2))
                if pv is not None:
                    kb.wait("tensor", pv[1], pv[0])
                mmv = None
                for kc in range(16):
                    mmv = kb.mm(pbank, buf[:, kc * 128:(kc + 1) * 128],
                                xnh[:, kc, tsl], kc == 0, kc == 15,
                                inc=(kc == 15))
                if kind == "k":
                    dst = kp[:, m, sh * 2048 + tb * 512: sh * 2048 + (tb + 1) * 512]
                    eng = "scalar" if tb % 2 == 0 else "vector"
                    sem = "sS" if tb % 2 == 0 else "sV"
                    kb.wait(eng, "sT", mmv)
                    if eng == "scalar":
                        ev = (kb.emit(eng, lambda e, d=dst, p=pbank:
                                      e.activation(d, p, AF.Copy), inc=sem), sem)
                    else:
                        ev = (kb.emit(eng, lambda e, d=dst, p=pbank:
                                      e.tensor_copy(d, p), inc=sem), sem)
                elif kind == "q":
                    dst = qp[:, m, tsl]
                    eng = "scalar" if tb % 2 == 0 else "vector"
                    sem = "sS" if tb % 2 == 0 else "sV"
                    kb.wait(eng, "sT", mmv)
                    if m >= 12:
                        kb.wait(eng, "sYS", T.get("p1q_store_last", 0))
                        kb.wait(eng, "sV", T.get("p1q_V", 0))
                        kb.wait(eng, "sT", T.get("p1q_T", 0))
                    if eng == "scalar":
                        ev = (kb.emit(eng, lambda e, d=dst, p=pbank:
                                      e.activation(d, p, AF.Copy), inc=sem), sem)
                    else:
                        ev = (kb.emit(eng, lambda e, d=dst, p=pbank:
                                      e.tensor_copy(d, p), inc=sem), sem)
                else:
                    vtile = sqt[tb % 2]
                    kb.wait("scalar", "sT", mmv)
                    kb.wait("scalar", "sV", T.get("vtile%d" % (tb % 2), 0))
                    ev0 = kb.emit("scalar", lambda e, d=vtile[:, :], p=pbank:
                                  e.activation(d, p, AF.Copy), inc="sS")
                    ev = (ev0, "sS")
                    kb.wait("tensor", "sS", ev0)
                    kb.wait("tensor", "sV", T.get("psT_free", 0))
                    tv = None
                    for jj in range(4):
                        tv = kb.emit("tensor", lambda e,
                                     oo=psT[:, jj * 128:(jj + 1) * 128],
                                     ii=vtile[:, jj * 128:(jj + 1) * 128]:
                                     e.transpose(oo, ii, eye_b[:, :]),
                                     inc="sT" if jj == 3 else None)
                    dst = vt[:, sh * 16 + tb * 4: sh * 16 + (tb + 1) * 4,
                             m * 128:(m + 1) * 128]
                    kb.wait("vector", "sT", tv)
                    cpv = kb.emit("vector", lambda e, d=dst,
                                  s=psT[:, 0:512].rearrange("p (a b) -> p a b", a=4, b=128):
                                  e.tensor_copy(d, s), inc="sV")
                    T["psT_free"] = cpv
                    T["vtile%d" % (tb % 2)] = cpv
                T["pevac%d" % (tb % 2)] = ev
            T["wbuf%d" % (ji % 2)] = kb.n["sT"]

    def rope(kind):
        nch = 4 if kind == "k" else 16
        buf3 = kp if kind == "k" else qp
        sc, bi = (C_K_SC, C_K_BI) if kind == "k" else (C_Q_SC, C_Q_BI)
        for tb in range(8 if kind == "k" else 4):
            tsl = slice(tb * 512, (tb + 1) * 512)
            kb.wait("sync", "sV", T.get("rope_cs", 0))
            kb.dma("sync", cos_sl[:, :], cost.ap()[:, tsl], "sYL")
            ls = kb.dma("sync", sin_sl[:, :], sint.ap()[:, tsl], "sYL")
            smv = None
            for ch in range(nch):
                kb.wait("vector", "sT", T.get("rope_sq", 0))
                qv = kb.emit("vector", lambda e, a=buf3[:, ch, tsl], q=rope_sq[:, :]:
                             e.tensor_tensor(q, a, a, ALU.mult), inc="sV")
                kb.wait("tensor", "sV", qv)
                smv = kb.mm(psB[0:1, 0:512], ones1_b[:, :], rope_sq[:, :],
                            ch == 0, ch == nch - 1)
                T["rope_sq"] = smv
            kb.wait("scalar", "sT", smv)
            sv = kb.emit("scalar", lambda e, s=sc, b=bi: e.activation(
                m_row[:, :], psB[0:1, 0:512], AF.Sqrt, bias=b, scale=s), inc="sS")
            kb.wait("vector", "sS", sv)
            rv = kb.emit("vector", lambda e: e.reciprocal(m_row[:, :], m_row[:, :]),
                         inc="sV")
            kb.wait("tensor", "sV", rv)
            kb.wait("tensor", "sS", sv)
            bv = kb.mm(psA[:, 1024:1536], onesk1_f[:, :], m_row[:, :], True, True)
            kb.wait("vector", "sT", bv)
            kb.wait("vector", "sYL", ls)
            cev = kb.emit("vector", lambda e: e.tensor_tensor(
                psB[:, 512:1024], cos_sl[:, :], psA[:, 1024:1536], ALU.mult), inc="sV")
            sev = kb.emit("vector", lambda e: e.tensor_tensor(
                se_t[:, :], sin_sl[:, :], psA[:, 1024:1536], ALU.mult), inc="sV")
            T["rope_cs"] = sev
            for ch in range(nch):
                sl = buf3[:, ch, tsl]
                kb.wait("tensor", "sV", T.get("rope_w", 0))
                rotv = kb.mm(psA[:, (ch % 2) * 512:(ch % 2) * 512 + 512],
                             rmat_b[:, :], sl, True, True)
                kb.wait("vector", "sV", cev)
                kb.emit("vector", lambda e, a=sl: e.tensor_tensor(
                    t1_t[:, :], a, psB[:, 512:1024], ALU.mult), inc="sV")
                kb.wait("vector", "sT", rotv)
                kb.emit("vector", lambda e, c=ch: e.tensor_tensor(
                    t2_t[:, :], psA[:, (c % 2) * 512:(c % 2) * 512 + 512],
                    se_t[:, :], ALU.mult), inc="sV")
                wv_ = kb.emit("vector", lambda e, a=sl: e.tensor_tensor(
                    a, t1_t[:, :], t2_t[:, :], ALU.add), inc="sV")
                T["rope_w"] = wv_

    for sh in (0, 1):
        ln_half(sh)
        kb.barrier()
        proj_half(sh)
        kb.barrier()
    kb.wait("gpsimd", "sYS", T["p1q_store_last"])
    kb.emit("gpsimd", lambda e: e.collective_compute(
        "AllGather", ALU.bypass, replica_groups=[list(range(8))],
        ins=[q_in.ap().opt()], outs=[q_out.ap().opt()]), inc="sC")
    kb.emit("gpsimd", lambda e: e.collective_compute(
        "AllReduce", ALU.add, replica_groups=[list(range(8))],
        ins=[bar_in.ap().opt()], outs=[bar_out.ap().opt()]), inc="sC")
    if d_qo is not None:
        kb.wait("gpsimd", "sC", 2)
        for i_ in range(6):
            kb.dma("gpsimd", d_qo.ap()[i_ * 32:(i_ + 1) * 32],
                   q_out.ap()[i_ * 32:(i_ + 1) * 32], "sG")
    rope("k")
    rope("q")
    kb.barrier()

    if d_kqv is not None:
        d_kp, d_qp, d_vt = d_kqv
        for m in range(4):
            kb.dma("sync", d_kp.ap()[m], kp[:, m, :], "sYS")
        for m in range(16):
            kb.dma("sync", d_qp.ap()[m], qp[:, m, :], "sYS")
        for j in range(32):
            kb.dma("sync", d_vt.ap()[j], vt[:, j, :], "sYS")
        kb.barrier()

    E_s = view(AR1, 0, [128, 32, 512], BF16)
    enc_qb = view(AR1, 32768, [128, 16, 512], BF16)
    rb_s = view(AR5, 0, [128, 512], F32)
    rs_row = view(AR5, 2048, [1, 512], F32)
    rr_row = view(AR5, 4096, [1, 512], F32)

    def pv_pair(g, n, g_kv):
        kb.wait("tensor", "sS", T["exp%d" % g])
        if g == 0:
            kb.wait("tensor", "sV", T.get("enc_done", 0))
        for jj in range(2):
            j = 2 * g + jj
            kb.mm(psB[:, 0:512], vt[:, j, g_kv * 128:(g_kv + 1) * 128],
                  E_s[:, j, :], j == 0, j == 31, inc=False)

    pending_norm = []

    def flush_norm():
        while pending_norm:
            fn = pending_norm.pop(0)
            fn()

    for qb in range(4):
        qs = slice(qb * 512, (qb + 1) * 512)
        for n in range(16):
            g_kv = n // 4
            prev_rsA = T.get("rs_endA", 0)
            prev_rsB = T.get("rs_endB", 0)
            for g in range(16):
                grp = psA[:, (g % 2) * 1024:(g % 2) * 1024 + 1024]
                kb.wait("tensor", "sS", T.get("qkgrp%d" % (g % 2), 0))
                for jj in range(2):
                    j = 2 * g + jj
                    kb.mm(grp[:, jj * 512:(jj + 1) * 512],
                          kp[:, g_kv, j * 128:(j + 1) * 128], qp[:, n, qs],
                          True, True, inc=(jj == 1))
                if g == 0:
                    flush_norm()
                kb.wait("scalar", "sT", kb.n["sT"])
                kb.wait("scalar", "sT", prev_rsA if g < 8 else prev_rsB)
                ev = kb.emit("scalar", lambda e, g_=g, gr=grp: e.activation(
                    E_s[:, 2 * g_:2 * g_ + 2, :].rearrange("p a b -> p (a b)"),
                    gr, AF.Exp, bias=C_EXP_BIAS), inc="sS")
                T["exp%d" % g] = ev
                T["qkgrp%d" % (g % 2)] = ev
                if g >= 1:
                    pv_pair(g - 1, n, g_kv)
            pv_pair(15, n, g_kv)
            kb.wait("tensor", "sV", T.get("rs_done", 0))
            for j in range(16):
                kb.mm(psB[0:1, 512:1024], ones1_b[:, :], E_s[:, j, :],
                      j == 0, False)
            T["rs_endA"] = kb.n["sT"]
            for j in range(16, 32):
                kb.mm(psB[0:1, 512:1024], ones1_b[:, :], E_s[:, j, :],
                      False, j == 31)
            T["rs_endB"] = kb.n["sT"]
            kb.wait("vector", "sT", kb.n["sT"])
            kb.wait("vector", "sS", T.get("rs_rowfree", 0))
            cpv = kb.emit("vector", lambda e: e.tensor_copy(
                rs_row[:, :], psB[0:1, 512:1024]), inc="sV")
            T["rs_done"] = cpv
            kb.wait("scalar", "sV", cpv)
            kb.emit("scalar", lambda e: e.activation(
                rr_row[:, :], rs_row[:, :], AF.Ln), inc="sS")
            rv = kb.emit("scalar", lambda e: e.activation(
                rr_row[:, :], rr_row[:, :], AF.Exp, scale=-1.0), inc="sS")
            T["rs_rowfree"] = rv

            def mk_norm(n_=n, rv_=rv):
                def do():
                    kb.wait("tensor", "sS", rv_)
                    bv = kb.mm(psB[:, 1024:1536], onesk1_f[:, :], rr_row[:, :],
                               True, True)
                    kb.wait("vector", "sT", bv)
                    kb.emit("vector", lambda e: e.tensor_copy(
                        rb_s[:, :], psB[:, 1024:1536]), inc="sV")
                    env = kb.emit("vector", lambda e: e.tensor_tensor(
                        enc_qb[:, n_, :], psB[:, 0:512], rb_s[:, :], ALU.mult),
                        inc="sV")
                    T["enc_done"] = env
                return do
            pending_norm.append(mk_norm())
        flush_norm()
        kb.wait("sync", "sV", kb.n["sV"])
        kb.dma("sync", s_enc.ap().transpose([1, 0, 2])[:, :, qs],
               enc_qb[:, :, :], "sYS")
    kb.barrier()

    enc_full = view(AR1, 0, [128, 16, 2048], BF16)
    wobuf = [view(AR3, 0, [128, 2048], BF16), view(AR3, 4096, [128, 2048], BF16)]
    attn_t = [view(AR5, 8192, [128, 512], BF16), view(AR5, 9216, [128, 512], BF16)]
    lenc = kb.dma("sync", enc_full[:, :, :], s_enc.ap().transpose([1, 0, 2]), "sYL")
    for m in range(16):
        kb.wait("gpsimd", "sT", T.get("wobuf%d" % (m % 2), 0))
        ldv = kb.dma("gpsimd", wobuf[m % 2][:, :],
                     wo.ap()[m].rearrange("p a b -> p (a b)"), "sG")
        for qb in range(4):
            pbank = psA[:, (qb % 2) * 512:(qb % 2) * 512 + 512]
            kb.wait("tensor", "sG", ldv)
            kb.wait("tensor", "sYL", lenc)
            kb.wait("tensor", "sS", T.get("aevac%d" % (qb % 2), 0))
            mmv = None
            for kc in range(16):
                mmv = kb.mm(pbank, wobuf[m % 2][:, kc * 128:(kc + 1) * 128],
                            enc_full[:, kc, qb * 512:(qb + 1) * 512], kc == 0,
                            kc == 15, inc=(kc == 15))
            kb.wait("scalar", "sT", mmv)
            kb.wait("scalar", "sYS", T.get("ast%d" % (qb % 2), 0))
            ev = kb.emit("scalar", lambda e, d=attn_t[qb % 2][:, :], p=pbank:
                         e.activation(d, p, AF.Copy), inc="sS")
            T["aevac%d" % (qb % 2)] = ev
            kb.wait("sync", "sS", ev)
            stv = kb.dma("sync", s_attn.ap()[m][:, qb * 512:(qb + 1) * 512],
                         attn_t[qb % 2][:, :], "sYS")
            T["ast%d" % (qb % 2)] = stv
        T["wobuf%d" % (m % 2)] = kb.n["sT"]
    kb.barrier()

    h_lo = view(AR1, 0, [128, 32, 1024], BF16)
    h_hi = view(AR2, 0, [128, 32, 1024], BF16)
    xn_th = view(AR4, 0, [128, 16, 1024], BF16)
    gbuf = [view(AR3, i * 4096, [128, 2048], BF16) for i in range(3)]
    ubuf = [view(AR3, 12288 + i * 4096, [128, 2048], BF16) for i in range(3)]
    sg_t = view(AR3, 24576, [128, 512], BF16)
    hq_t = view(AR3, 25600, [128, 512], BF16)
    ib_s = view(AR5, 0, [128, 1024], F32)
    of_s = view(AR5, 4096, [128, 1024], F32)
    at_t = view(AR5, 8192, [128, 1024], BF16)
    x_t = view(AR5, 10240, [128, 1024], F32)
    ih_row = view(AR5, 10240, [1, 1024], F32)

    def h_slice(mp, qoff, th_unused=None):
        base = h_lo if mp < 32 else h_hi
        return base[:, mp % 32, qoff:qoff + 512]

    for th in range(2):
        tsl = slice(th * 1024, (th + 1) * 1024)
        kb.wait("sync", "sT", T.get("xnth_free", 0))
        lxn = kb.dma("sync", xn_th[:, :, :],
                     s_xn.ap().transpose([1, 0, 2])[:, :, tsl], "sYL")
        for mp in range(64):
            bg, bu = gbuf[mp % 3], ubuf[mp % 3]
            kb.wait("sync", "sT", T.get("gumm%d" % (mp % 3), 0))
            kb.wait("gpsimd", "sT", T.get("gumm%d" % (mp % 3), 0))
            if th == 0 and mp == 0:
                kb.wait("sync", "sC", 2)
                kb.wait("gpsimd", "sC", 2)
            lg = kb.dma("sync", bg[:, :], q_out.ap()[mp], "sYL")
            lu = kb.dma("gpsimd", bu[:, :], q_out.ap()[64 + mp], "sG")
            kb.wait("tensor", "sYL", lg)
            kb.wait("tensor", "sG", lu)
            kb.wait("tensor", "sYL", lxn)
            for part, wt, bb, qoff in (("u0", bu, 2, 0), ("u1", bu, 3, 512),
                                       ("g0", bg, 0, 0), ("g1", bg, 1, 512)):
                pbank = psA[:, bb * 512:(bb + 1) * 512]
                kb.wait("tensor", "sV", T.get("hfree%d" % bb, 0))
                kb.wait("tensor", "sS", T.get("sfree%d" % bb, 0))
                for kc in range(16):
                    kb.mm(pbank, wt[:, kc * 128:(kc + 1) * 128],
                          xn_th[:, kc, qoff:qoff + 512], kc == 0, kc == 15,
                          inc=(kc == 15))
                T["gum_%s" % part] = kb.n["sT"]
            for ih in range(2):
                bg_b, bu_b = ih, 2 + ih
                qoff = ih * 512
                kb.wait("scalar", "sT", T["gum_g%d" % ih])
                kb.wait("scalar", "sV", T.get("sg_free", 0))
                sgv = kb.emit("scalar", lambda e, b=bg_b: e.activation(
                    sg_t[:, :], psA[:, b * 512:(b + 1) * 512], AF.Silu), inc="sS")
                T["sfree%d" % bg_b] = sgv
                kb.wait("vector", "sS", sgv)
                kb.wait("vector", "sT", T["gum_u%d" % ih])
                hv = kb.emit("vector", lambda e, b=bu_b, d=h_slice(mp, qoff):
                             e.tensor_tensor(d, sg_t[:, :],
                                             psA[:, b * 512:(b + 1) * 512], ALU.mult),
                             inc="sV")
                T["hfree%d" % bu_b] = hv
                T["hfree%d" % bg_b] = hv
                T["sg_free"] = hv
                kb.wait("vector", "sT", T.get("hq_free", 0))
                hqv = kb.emit("vector", lambda e, d=h_slice(mp, qoff):
                              e.tensor_tensor(hq_t[:, :], d, d, ALU.mult), inc="sV")
                kb.wait("tensor", "sV", hqv)
                hsv = kb.mm(psB[0:1, qoff:qoff + 512], ones1_b[:, :], hq_t[:, :],
                            mp == 0, mp == 63)
                T["hq_free"] = hsv
            T["gumm%d" % (mp % 3)] = kb.n["sT"]
            if d_h is not None and th == 0:
                kb.wait("sync", "sV", kb.n["sV"])
                kb.dma("sync", d_h.ap()[mp], h_slice(mp, 0, None).tensor[
                    :, mp % 32, :] if False else h_lo[:, mp, :] if mp < 32
                    else h_hi[:, mp - 32, :], "sYS")
        kb.barrier()
        sv = kb.emit("scalar", lambda e: e.activation(
            ih_row[:, :], psB[0:1, 0:1024], AF.Sqrt, bias=C_H_BI, scale=C_H_SC),
            inc="sS")
        kb.wait("vector", "sS", sv)
        rv = kb.emit("vector", lambda e: e.reciprocal(ih_row[:, :], ih_row[:, :]),
                     inc="sV")
        kb.wait("tensor", "sV", rv)
        kb.mm(psB[:, 0:512], onesk1_f[:, :], ih_row[:, 0:512], True, True)
        b2 = kb.mm(psB[:, 512:1024], onesk1_f[:, :], ih_row[:, 512:1024], True, True)
        kb.wait("vector", "sT", b2)
        kb.emit("vector", lambda e: e.tensor_copy(ib_s[:, :], psB[:, 0:1024]), inc="sV")
        kb.barrier()
        wp_sw = view(AR3, 0, [128, 64, 256], BF16)
        for sw in range(16):
            par = sw % 2
            if par == 0:
                kb.wait("sync", "sT", T.get("sw_free1", 0))
                lw = kb.dma("sync", wp_sw[:, :, :],
                            q_out.ap()[128:192, :, (sw // 2) * 256:(sw // 2 + 1) * 256]
                            .transpose([1, 0, 2]), "sYL")
                T["wpsw_ld"] = lw
            kb.wait("tensor", "sYL", T["wpsw_ld"])
            kb.wait("tensor", "sV", T.get("of_done%d" % par, 0))
            for kc in range(64):
                for t2 in range(2):
                    kb.mm(psA[:, par * 1024 + t2 * 512: par * 1024 + (t2 + 1) * 512],
                          wp_sw[:, kc, par * 128:(par + 1) * 128],
                          h_slice(kc, t2 * 512), kc == 0, kc == 63,
                          inc=(kc == 63 and t2 == 1))
            T["sw_free%d" % par] = kb.n["sT"]
            mg = sw
            kb.wait("sync", "sV", T.get("at_free", 0))
            la = kb.dma("sync", at_t[:, :], s_attn.ap()[mg][:, tsl], "sYL")
            lx = kb.dma("sync", x_t[:, :], xt.ap()[mg * 128:(mg + 1) * 128, tsl],
                        "sYL")
            kb.wait("vector", "sT", T["sw_free%d" % par])
            kb.wait("vector", "sYS", T.get("o_st", 0))
            kb.emit("vector", lambda e, p_=par: e.tensor_tensor(
                of_s[:, :], psA[:, p_ * 1024:(p_ + 1) * 1024], ib_s[:, :],
                ALU.mult), inc="sV")
            kb.wait("vector", "sYL", la)
            av2 = kb.emit("vector", lambda e: e.tensor_tensor(
                of_s[:, :], of_s[:, :], at_t[:, :], ALU.add), inc="sV")
            T["at_free"] = av2
            kb.wait("vector", "sYL", lx)
            ofv = kb.emit("vector", lambda e: e.tensor_tensor(
                of_s[:, :], of_s[:, :], x_t[:, :], ALU.add), inc="sV")
            T["of_done%d" % par] = ofv
            kb.wait("sync", "sV", ofv)
            ostv = kb.dma("sync", o.ap()[mg * 128:(mg + 1) * 128, tsl],
                          of_s[:, :], "sYS")
            T["o_st"] = ostv
            T["xnth_free"] = kb.n["sT"]
        kb.barrier()

    kb.wait("sync", "sYS", kb.n["sYS"])
    kb.emit("sync", lambda e: e.nop())
    kb.wait("gpsimd", "sYS", kb.n["sYS"])
    kb.emit("gpsimd", lambda e: e.nop())

    with es:
        sems = {s: es.enter_context(nc.semaphore(s)) for s in
                ("sG", "sYL", "sYS", "sT", "sV", "sS", "sC")}
        with nc.Block() as block:
            kb.replay(block, sems)
    return nc


def _repack_proj(w):
    cin, cout = w.shape
    r = np.asarray(w, np.float32).reshape(cin // 128, 128, cout // 128, 128)
    return np.ascontiguousarray(r.transpose(2, 1, 0, 3))


def _repack_gu(w):
    h, d = w.shape
    r = np.asarray(w, np.float32).reshape(h // 128, 128, d // 128, 128)
    return np.ascontiguousarray(r.transpose(0, 3, 2, 1))


def _repack_wp(w):
    dout, hin = w.shape
    r = np.asarray(w, np.float32).reshape(dout // 128, 128, hin // 128, 128)
    return np.ascontiguousarray(r.transpose(2, 3, 0, 1))


def _tables(T=4096, H=128):
    fraction = np.arange(0, H, 2, dtype=np.float64) / H
    timescale = 10000.0 ** fraction
    sinusoid = np.arange(T, dtype=np.float64)[:, None] / timescale[None, :]
    sinusoid = np.concatenate([sinusoid, sinusoid], axis=-1)
    return (np.ascontiguousarray(np.sin(sinusoid).T.astype(np.float32)),
            np.ascontiguousarray(np.cos(sinusoid).T.astype(np.float32)))


def kernel(L0_attn_x, L0_attn_Wq, L0_attn_Wk, L0_attn_Wv, L0_attn_Wout,
           L0_ffn_w_gate_up, L0_ffn_w_proj):
    x = np.asarray(L0_attn_x, dtype=np.float32)
    B, Tt, C = x.shape
    xt_full = np.ascontiguousarray(x.reshape(B * Tt, C).T)

    shared = {
        "wq": _repack_proj(L0_attn_Wq), "wk": _repack_proj(L0_attn_Wk),
        "wv": _repack_proj(L0_attn_Wv), "wo": _repack_proj(L0_attn_Wout),
        "ones1": np.ones((128, 1), np.float32),
        "onesk1": np.ones((1, 128), np.float32),
        "eye": np.eye(128, dtype=np.float32),
    }
    wgu_r = _repack_gu(L0_ffn_w_gate_up).reshape(128, 128, 2048)
    wp_r = _repack_wp(L0_ffn_w_proj).reshape(64, 128, 2048)
    wall = np.concatenate([wgu_r, wp_r], axis=0)
    wshards = [np.ascontiguousarray(wall[24 * c:24 * (c + 1)]) for c in range(8)]
    sel2 = np.zeros((2, 128), np.float32)
    sel2[0, :64] = 1.0
    sel2[1, 64:] = 1.0
    segc = np.zeros((128, 2), np.float32)
    segc[:64, 0] = 1.0
    segc[64:, 1] = 1.0
    rmat = np.zeros((128, 128), np.float32)
    for m in range(64):
        rmat[m + 64, m] = -1.0
        rmat[m, m + 64] = 1.0
    cvec = np.zeros((128, 16), np.float32)
    for i, c in enumerate(CONSTS):
        cvec[:, i] = c
    shared.update({"sel2": sel2, "segc": segc, "rmat": rmat, "cvec": cvec})
    sint_t, cost_t = _tables()

    nc = build(_DBG)
    in_maps = []
    for core in range(8):
        b, h = core // 2, core % 2
        base = b * Tt
        own = xt_full[:, base + h * 2048: base + (h + 1) * 2048]
        par = xt_full[:, base + (1 - h) * 2048: base + (2 - h) * 2048]
        pos = np.concatenate([np.arange(h * 2048, (h + 1) * 2048),
                              np.arange((1 - h) * 2048, (2 - h) * 2048)])
        im = dict(shared)
        im["wshard"] = wshards[core]
        im["xt"] = np.ascontiguousarray(np.concatenate([own, par], axis=1))
        im["cost"] = np.ascontiguousarray(cost_t[:, pos])
        im["sint"] = np.ascontiguousarray(sint_t[:, pos])
        in_maps.append(im)
    res = run_bass_kernel_spmd(nc, in_maps, core_ids=list(range(8)))
    out = np.empty((B, Tt, C), np.float32)
    for core in range(8):
        b, h = core // 2, core % 2
        out[b, h * 2048:(h + 1) * 2048, :] = res.results[core]["o"].T
    return out
